# version 14
# baseline (speedup 1.0000x reference)
import numpy as np
from contextlib import ExitStack

import concourse.bass as bass
import concourse.mybir as mybir
from concourse.bass_utils import run_bass_kernel_spmd

F32 = mybir.dt.float32
BF16 = mybir.dt.bfloat16
I32 = mybir.dt.int32
AF = mybir.ActivationFunctionType
ALU = mybir.AluOpType
AX = mybir.AxisListType

C = 2048
NCT = 16
NH = 16
HD = 128
NST = 8
TL = 1024
NE = 16
NI = 1024
TOPK = 4
EPS = 1e-5
SCALE = float(HD) ** -0.5
MDL = 8192

ENG = ("pe", "dve", "act", "pool", "sp")


class Sched:
    def __init__(self):
        self.items = {n: [] for n in ENG}
        self.waited = {n: {} for n in ENG}
        self.cnt = {}
        self.last = {}
        self.readers = {}
        self.dma_sems = set()

    def _need(self, reads, writes):
        need = {}

        def add(sem, val):
            if val > need.get(sem, 0):
                need[sem] = val

        for t in reads:
            lw = self.last.get(t)
            if lw:
                add(*lw)
        for t in writes:
            lw = self.last.get(t)
            if lw:
                add(*lw)
            for sem, val in self.readers.get(t, {}).items():
                add(sem, val)
        return need

    def _waits(self, st, need):
        for sem in sorted(need):
            val = need[sem]
            if self.waited[st].get(sem, 0) < val:
                self.items[st].append(("w", sem, val))
                self.waited[st][sem] = val

    def group(self, st, fns, reads=(), writes=()):
        assert st in ("pe", "dve", "act", "pool")
        self._waits(st, self._need(reads, writes))
        self.cnt[st] = self.cnt.get(st, 0) + 1
        v = self.cnt[st]
        for f in fns[:-1]:
            self.items[st].append(("op", f, None, 0))
        self.items[st].append(("op", fns[-1], st, 1))
        for t in writes:
            self.last[t] = (st, v)
            self.readers[t] = {}
        for t in reads:
            self.readers.setdefault(t, {})[st] = v

    def op(self, st, fn, reads=(), writes=()):
        self.group(st, [fn], reads, writes)

    def dma(self, q, sem, fn, reads=(), writes=()):
        assert q in ("sp", "pool")
        self.dma_sems.add(sem)
        self._waits(q, self._need(reads, writes))
        self.cnt[sem] = self.cnt.get(sem, 0) + 16
        v = self.cnt[sem]
        self.items[q].append(("op", fn, sem, 16))
        for t in writes:
            self.last[t] = (sem, v)
            self.readers[t] = {}
        for t in reads:
            self.readers.setdefault(t, {})[sem] = v

    def level(self, sem, tokens):
        for t in tokens:
            if t in self.last:
                self.last[t] = (sem, self.cnt.get(sem, 0))

    def snap(self):
        return {s: self.cnt.get(s, 0) for s in ("pe", "dve", "act", "pool")}

    def barrier_at(self, q, snap):
        self._waits(q, {s: v for s, v in snap.items() if v})

    def barrier_queue(self, q):
        need = {}
        for s in ("pe", "dve", "act", "pool"):
            if self.cnt.get(s, 0):
                need[s] = self.cnt[s]
        for s in self.dma_sems:
            if self.cnt.get(s, 0):
                need[s] = self.cnt[s]
        self._waits(q, need)

    def tail_wait(self, q, sem):
        self.waited[q].pop(sem, None)
        self._waits(q, {sem: self.cnt.get(sem, 0)})

    def flush(self, nc, semh):
        if not any(self.items[n] for n in ENG):
            return
        with nc.Block() as blk:
            for st, attr in (("pe", "tensor"), ("dve", "vector"),
                             ("act", "scalar"), ("pool", "gpsimd"),
                             ("sp", "sync")):
                items = self.items[st]
                self.items[st] = []
                if not items:
                    continue

                def mk(items=items):
                    def f(eng):
                        for it in items:
                            if it[0] == "w":
                                eng.wait_ge(semh[it[1]], it[2])
                            else:
                                inst = it[1](eng)
                                if it[2] is not None:
                                    inst.then_inc(semh[it[2]], it[3])
                    return f

                getattr(blk, attr)(mk())


ALL_SEMS = (["pe", "dve", "act", "pool", "sm", "smp", "xr", "gw", "bo",
             "xkv0", "xkv1", "xkv2", "wqb", "wkb", "wvb", "wo0", "wo1",
             "wg", "wu", "wd0", "wd1", "out"])


def build_program():
    nc = bass.Bass("TRN2", target_bir_lowering=False)

    d_xkv = nc.dram_tensor("xkv", [2048, C], F32, kind="ExternalInput").ap()
    d_mask = nc.dram_tensor("mask", [128, 256], F32, kind="ExternalInput").ap()
    d_wq = nc.dram_tensor("wq", [NH, 128, NCT, HD], F32, kind="ExternalInput").ap()
    d_wk = nc.dram_tensor("wk", [NH, 128, NCT, HD], F32, kind="ExternalInput").ap()
    d_wv = nc.dram_tensor("wv", [NH, 128, NCT, HD], F32, kind="ExternalInput").ap()
    d_bqt = nc.dram_tensor("bqt", [HD, NH], F32, kind="ExternalInput").ap()
    d_bkt = nc.dram_tensor("bkt", [HD, NH], F32, kind="ExternalInput").ap()
    d_bvt = nc.dram_tensor("bvt", [HD, NH], F32, kind="ExternalInput").ap()
    d_wo = nc.dram_tensor("wo", [128, NCT, C], F32, kind="ExternalInput").ap()
    d_bo = nc.dram_tensor("bo", [1, C], F32, kind="ExternalInput").ap()
    d_gwt = nc.dram_tensor("gwt", [128, NCT, NE], F32, kind="ExternalInput").ap()
    d_ebr = nc.dram_tensor("ebr", [1, NE], F32, kind="ExternalInput").ap()
    d_wgt = nc.dram_tensor("wgt", [NE, 128, NCT, NI], F32, kind="ExternalInput").ap()
    d_wut = nc.dram_tensor("wut", [NE, 128, NCT, NI], F32, kind="ExternalInput").ap()
    d_wdt = nc.dram_tensor("wdt", [NE, 128, 8, C], F32, kind="ExternalInput").ap()
    d_swgt = nc.dram_tensor("swgt", [128, NCT, NI], F32, kind="ExternalInput").ap()
    d_swut = nc.dram_tensor("swut", [128, NCT, NI], F32, kind="ExternalInput").ap()
    d_swdt = nc.dram_tensor("swdt", [128, 8, C], F32, kind="ExternalInput").ap()
    d_out = nc.dram_tensor("out", [TL, C], F32, kind="ExternalOutput").ap()

    S = Sched()

    with ExitStack() as top:
        semh = {n: top.enter_context(nc.semaphore(f"s_{n}")) for n in ALL_SEMS}
        E = top.enter_context

        x_res = E(nc.sbuf_tensor("x_res", [128, NST, C], F32))
        ident = E(nc.sbuf_tensor("ident", [128, 128], BF16))
        cst = E(nc.sbuf_tensor("cst", [128, 3], F32))
        onr = E(nc.sbuf_tensor("onr", [1, 128], BF16))
        st_ln = E(nc.sbuf_tensor("st_ln", [128, 16], F32))
        st_at = E(nc.sbuf_tensor("st_at", [128, 24], F32))
        bnr = E(nc.sbuf_tensor("bnr", [128, 48], F32))

        with ExitStack() as sAB:
            EB = sAB.enter_context
            YT = EB(nc.sbuf_tensor("YT", [128, NH, TL], BF16))

            with ExitStack() as s1:
                E1 = s1.enter_context
                ln1T = E1(nc.sbuf_tensor("ln1T", [128, NCT, 2048], BF16))
                wqb = E1(nc.sbuf_tensor("wqb", [128, NCT, HD], BF16))
                wkb = E1(nc.sbuf_tensor("wkb", [128, NCT, HD], BF16))
                wvb = E1(nc.sbuf_tensor("wvb", [128, NCT, HD], BF16))
                mask_sb = E1(nc.sbuf_tensor("mask_sb", [128, 256], BF16))
                bqt_s = E1(nc.sbuf_tensor("bqt_s", [HD, NH], F32))
                bkt_sb = E1(nc.sbuf_tensor("bkt_sb", [HD, NH], F32))
                bvt_sb = E1(nc.sbuf_tensor("bvt_sb", [HD, NH], F32))

                with ExitStack() as p1:
                    EP = p1.enter_context
                    io32 = EP(nc.sbuf_tensor("io32", [128, 128], I32))
                    bqt_r = EP(nc.sbuf_tensor("bqt_r", [HD, NH], F32))
                    xsl = [EP(nc.sbuf_tensor(f"xsl{s}", [128, 2048], F32)) for s in range(3)]
                    lnb = [EP(nc.sbuf_tensor(f"lnb{s}", [128, 2048], BF16)) for s in range(2)]
                    psLN = [EP(nc.psum_tensor(f"psLN{j}", [128, 1024], BF16)) for j in range(4)]

                    S.dma("pool", "smp", lambda e, o=mask_sb, i=d_mask: e.dma_start(out=o[:], in_=i[:]), writes=[("mask",)])
                    S.dma("sp", "sm", lambda e, o=bqt_r, i=d_bqt: e.dma_start(out=o[:], in_=i[:]), writes=[("bqtr",)])
                    S.dma("sp", "sm", lambda e, o=bkt_sb, i=d_bkt: e.dma_start(out=o[:], in_=i[:]), writes=[("bkt",)])
                    S.dma("sp", "sm", lambda e, o=bvt_sb, i=d_bvt: e.dma_start(out=o[:], in_=i[:]), writes=[("bvt",)])
                    S.level("sm", [("bqtr",), ("bkt",), ("bvt",)])
                    S.level("smp", [("mask",)])
                    S.op("pool", lambda e, a=io32: e.iota(a[:], [[1, 128]], base=0, channel_multiplier=-1),
                         writes=[("io32",)])
                    S.op("dve", lambda e, a=ident, b=io32: e.tensor_scalar(a[:], b[:], 0.0, None, ALU.is_equal),
                         reads=[("io32",)], writes=[("ident",)])
                    S.op("dve", lambda e, a=cst: e.memset(a[:, 0:1], EPS), writes=[("cst",)])
                    S.op("dve", lambda e, a=cst: e.memset(a[:, 1:2], 0.0), reads=[("cst",)], writes=[("cst",)])
                    S.op("dve", lambda e, a=cst: e.memset(a[:, 2:3], -20.0), reads=[("cst",)], writes=[("cst",)])
                    S.op("dve", lambda e, a=onr: e.memset(a[:], 1.0), writes=[("onr",)])
                    S.op("dve", lambda e, a=bqt_s, b=bqt_r: e.tensor_scalar(a[:], b[:], SCALE, None, ALU.mult),
                         reads=[("bqtr",)], writes=[("bqts",)])

                    for nm, dst, src in (("wqb", wqb, d_wq), ("wkb", wkb, d_wk), ("wvb", wvb, d_wv)):
                        S.dma("pool", nm,
                              lambda e, o=dst, i=src: e.dma_start(out=o[:], in_=i[0], max_dma_last_dim=MDL),
                              writes=[(nm,)])

                    def ln1_load(i):
                        s3 = i % 3
                        S.dma("sp", f"xkv{s3}",
                              lambda e, o=xsl[s3], i2=d_xkv, i=i: e.dma_start(out=o[:], in_=i2[i * 128:(i + 1) * 128, :]),
                              writes=[("xsl", s3)])

                    def ln1_stats(i):
                        sl = i % 2
                        s3 = i % 3
                        b = 8 * sl
                        bb = 24 * sl
                        for cc in range(4):
                            S.op("dve", lambda e, x=xsl[s3], bb=bb, cc=cc:
                                 e.bn_stats(bnr[:, bb + 6 * cc:bb + 6 * cc + 6], x[:, 512 * cc:512 * (cc + 1)]),
                                 reads=[("xsl", s3)], writes=[("bnr", sl, cc)])
                        S.op("dve", lambda e, bb=bb, b=b: e.bn_aggr(st_ln[:, b:b + 2], bnr[:, bb:bb + 24]),
                             reads=[("bnr", sl, 0), ("bnr", sl, 1), ("bnr", sl, 2), ("bnr", sl, 3)],
                             writes=[("lst", sl, 0)])
                        S.op("act", lambda e, b=b: e.activation(st_ln[:, b + 2:b + 3], st_ln[:, b + 1:b + 2],
                                                                AF.Ln, bias=cst[:, 0:1], scale=1.0),
                             reads=[("lst", sl, 0), ("cst",)], writes=[("lst", sl, 1)])
                        S.op("act", lambda e, b=b: e.activation(st_ln[:, b + 3:b + 4], st_ln[:, b + 2:b + 3],
                                                                AF.Exp, bias=cst[:, 1:2], scale=-0.5),
                             reads=[("lst", sl, 1), ("cst",)], writes=[("lst", sl, 1)])

                    def ln1_norm(i):
                        sl = i % 2
                        s3 = i % 3
                        b = 8 * sl
                        S.op("dve", lambda e, b=b: e.scalar_tensor_tensor(st_ln[:, b + 4:b + 5],
                                                                          st_ln[:, b:b + 1], -1.0,
                                                                          st_ln[:, b + 3:b + 4],
                                                                          ALU.mult, ALU.mult),
                             reads=[("lst", sl, 0), ("lst", sl, 1)], writes=[("lst", sl, 2)])
                        S.op("act", lambda e, o=lnb[sl], x=xsl[s3], b=b:
                             e.activation(o[:], x[:], AF.Identity, bias=st_ln[:, b + 4:b + 5],
                                          scale=st_ln[:, b + 3:b + 4]),
                             reads=[("xsl", s3), ("lst", sl, 1), ("lst", sl, 2)], writes=[("lnb", sl)])

                    def ln1_tp(i):
                        sl = i % 2
                        for j in range(2):
                            bj = 2 * (i % 2) + j
                            fns = []
                            for q in range(8):
                                fns.append(lambda e, p=psLN[bj], src=lnb[sl], q=q, j=j:
                                           e.transpose(p[:, q * 128:(q + 1) * 128],
                                                       src[:, (8 * j + q) * 128:(8 * j + q + 1) * 128], ident[:]))
                            S.group("pe", fns, reads=[("lnb", sl), ("ident",)], writes=[("psLN", bj)])

                    def ln1_ev(i):
                        for j in range(2):
                            bj = 2 * (i % 2) + j
                            S.op("dve", lambda e, p=psLN[bj], i=i, j=j:
                                 e.tensor_copy(ln1T[:, 8 * j:8 * (j + 1), i * 128:(i + 1) * 128],
                                               p[:].rearrange("p (a b) -> p a b", b=128)),
                                 reads=[("psLN", bj)], writes=[("lnT",)])

                    ln1_load(0)
                    ln1_load(1)
                    for i in range(16):
                        ln1_stats(i)
                        if i >= 1:
                            ln1_norm(i - 1)
                            ln1_tp(i - 1)
                            if i + 1 < 16:
                                ln1_load(i + 1)
                        if i >= 2:
                            ln1_ev(i - 2)
                    ln1_norm(15)
                    ln1_tp(15)
                    ln1_ev(14)
                    ln1_ev(15)

                    for k in range(NST):
                        S.dma("sp", "xr",
                              lambda e, o=x_res, i=d_xkv, k=k:
                              e.dma_start(out=o[:, k, :], in_=i[256 * k:256 * (k + 1):2, :]),
                              writes=[("xr", k)])
                    S.level("xr", [("xr", k) for k in range(NST)])



                    S.flush(nc, semh)

                with ExitStack() as at:
                    EA = at.enter_context
                    qt = EA(nc.sbuf_tensor("qt", [128, TL], BF16))
                    kt = EA(nc.sbuf_tensor("kt", [128, 2048], BF16))
                    vtt = EA(nc.sbuf_tensor("vtt", [128, 2048], BF16))
                    vsb = EA(nc.sbuf_tensor("vsb", [128, NCT, HD], BF16))
                    psb = [EA(nc.sbuf_tensor(f"psb{s}", [128, 2048], BF16)) for s in range(3)]
                    ptb = [EA(nc.sbuf_tensor(f"ptb{s}", [128, NCT, HD], BF16)) for s in range(2)]
                    psP = [EA(nc.psum_tensor(f"psP{j}", [128, 512], F32)) for j in range(2)]
                    psS = EA(nc.psum_tensor("psS", [128, 1536], F32))
                    psPT = [EA(nc.psum_tensor(f"psPT{j}", [128, 1024], BF16)) for j in range(2)]
                    psPV = EA(nc.psum_tensor("psPV", [128, 512], F32))

                    lnr = ln1T[:].rearrange("p c (k w) -> p c k w", w=256)
                    rb = [0]

                    def proj(h, wbuf, wtok, bias_ap, btok, scal, dst, dtok, nchunks, pp, stride2):
                        for ncx in range(nchunks):
                            p = psP[pp[0] % 2]
                            ptok = ("psP", pp[0] % 2)
                            pp[0] += 1
                            fns = []
                            for ct in range(NCT):
                                if stride2:
                                    rhs = lnr[:, ct, 4 * ncx:4 * ncx + 4, 0:256:2]
                                else:
                                    rhs = ln1T[:, ct, ncx * 512:(ncx + 1) * 512]
                                fns.append(lambda e, p=p, w=wbuf, ct=ct, rhs=rhs:
                                           e.matmul(p[:], w[:, ct, :], rhs,
                                                    start=(ct == 0), stop=(ct == NCT - 1)))
                            S.group("pe", fns, reads=[(wtok,), ("lnT",)], writes=[ptok])
                            S.op("act", lambda e, p=p, d=dst, ncx=ncx, b=bias_ap, sc=scal:
                                 e.activation(d[:, ncx * 512:(ncx + 1) * 512], p[:], AF.Identity,
                                              bias=b, scale=sc),
                                 reads=[ptok, btok], writes=[dtok + (ncx,)])

                    def s_front(k):
                        par = k % 2
                        pb = k % 3
                        base = 12 * par
                        L = 256 * (k + 1)
                        widths = [512] * (L // 512) + ([256] if L % 512 else [])
                        nchunks = len(widths)
                        off = 0
                        for ci, w in enumerate(widths):
                            bk = rb[0] % 3
                            rb[0] += 1
                            S.group("pe",
                                    [lambda e, bk=bk, w=w, off=off, k=k:
                                     e.matmul(psS[:, bk * 512:bk * 512 + w],
                                              qt[:, k * 128:(k + 1) * 128],
                                              kt[:, off:off + w], start=True, stop=True)],
                                    reads=[("qt", k // 4), ("kt", off // 512)], writes=[("psS", bk)])
                            if ci == nchunks - 1:
                                mo = bk * 512 + w - 256
                                if w == 512:
                                    S.op("dve", lambda e, bk=bk:
                                         e.tensor_scalar(psS[:, bk * 512:bk * 512 + 256],
                                                         psS[:, bk * 512:bk * 512 + 256],
                                                         90.0, None, ALU.min),
                                         reads=[("psS", bk)], writes=[("psS", bk)])
                                S.op("dve", lambda e, mo=mo:
                                     e.scalar_tensor_tensor(psS[:, mo:mo + 256], psS[:, mo:mo + 256],
                                                            90.0, mask_sb[:], ALU.min, ALU.add),
                                     reads=[("psS", bk), ("mask",)], writes=[("psS", bk)])
                            else:
                                S.op("dve", lambda e, bk=bk, w=w:
                                     e.tensor_scalar(psS[:, bk * 512:bk * 512 + w],
                                                     psS[:, bk * 512:bk * 512 + w],
                                                     90.0, None, ALU.min),
                                     reads=[("psS", bk)], writes=[("psS", bk)])
                            S.op("act", lambda e, bk=bk, w=w, off=off, base=base, ci=ci, pb=pb:
                                 e.activation(psb[pb][:, off:off + w], psS[:, bk * 512:bk * 512 + w], AF.Exp,
                                              bias=cst[:, 2:3], scale=1.0,
                                              accum_out=st_at[:, base + 5 + ci:base + 6 + ci]),
                                 reads=[("psS", bk), ("cst",)], writes=[("psb", pb), ("sum", par, ci)])
                            off += w

                    def s_back(k):
                        par = k % 2
                        pb = k % 3
                        base = 12 * par
                        L = 256 * (k + 1)
                        nchunks = len([512] * (L // 512) + ([256] if L % 512 else []))
                        S.op("dve", lambda e, base=base, nchunks=nchunks:
                             e.reduce_sum(st_at[:, base + 9:base + 10], st_at[:, base + 5:base + 5 + nchunks],
                                          axis=AX.X),
                             reads=[("sum", par, ci) for ci in range(nchunks)], writes=[("tsum", par)])
                        S.op("dve", lambda e, base=base: e.reciprocal(st_at[:, base + 10:base + 11],
                                                                     st_at[:, base + 9:base + 10]),
                             reads=[("tsum", par)], writes=[("at2", par)])
                        S.op("act", lambda e, L=L, base=base, pb=pb:
                             e.activation(psb[pb][:, 0:L], psb[pb][:, 0:L], AF.Identity,
                                          bias=cst[:, 1:2], scale=st_at[:, base + 10:base + 11]),
                             reads=[("at2", par), ("psb", pb), ("cst",)], writes=[("psb", pb)])

                    def pt_part(k):
                        pb = k % 3
                        tb = k % 2
                        nkc = 2 * (k + 1)
                        for ri, r in enumerate(range(0, nkc, 8)):
                            pt = psPT[ri % 2]
                            pttok = ("psPT", ri % 2)
                            n = min(8, nkc - r)
                            fns = [lambda e, j=j, r=r, pb=pb, pt=pt:
                                   e.transpose(pt[:, (j - r) * 128:(j - r + 1) * 128],
                                               psb[pb][:, j * 128:(j + 1) * 128], ident[:])
                                   for j in range(r, r + n)]
                            S.group("pe", fns, reads=[("psb", pb), ("ident",)], writes=[pttok])
                            S.op("act", lambda e, r=r, n=n, tb=tb, pt=pt:
                                 e.copy(ptb[tb][:, r:r + n, :],
                                        pt[:, 0:n * 128].rearrange("p (a b) -> p a b", b=128)),
                                 reads=[pttok], writes=[("ptb", tb)])

                    def pv_part(k, h):
                        tb = k % 2
                        nkc = 2 * (k + 1)
                        fns = [lambda e, j=j, k=k, nkc=nkc, tb=tb:
                               e.matmul(psPV[:, (k % 4) * 128:(k % 4 + 1) * 128],
                                        vsb[:, j, :], ptb[tb][:, j, :],
                                        start=(j == 0), stop=(j == nkc - 1))
                               for j in range(nkc)]
                        S.group("pe", fns, reads=[("vsb",), ("ptb", tb)], writes=[("psPV",)])
                        if k % 4 == 3:
                            S.op("act", lambda e, h=h, k=k:
                                 e.copy(YT[:, h, (k - 3) * 128:(k + 1) * 128], psPV[:]),
                                 reads=[("psPV",)], writes=[("yt",)])

                    vtt_toks = [("vtt", j) for j in range(4)]

                    def projall(h):
                        pp = [0]
                        proj(h, wqb, "wqb", bqt_s[:, h:h + 1], ("bqts",), SCALE, qt, ("qt",), 2, pp, True)
                        proj(h, wkb, "wkb", bkt_sb[:, h:h + 1], ("bkt",), 1.0, kt, ("kt",), 4, pp, False)
                        proj(h, wvb, "wvb", bvt_sb[:, h:h + 1], ("bvt",), 1.0, vtt, ("vtt",), 4, pp, False)

                    def vtrans(h):
                        for ri, r in enumerate(range(0, 16, 8)):
                            pt = psPT[ri % 2]
                            pttok = ("psPT", ri % 2)
                            fns = [lambda e, j=j, r=r, pt=pt: e.transpose(pt[:, (j - r) * 128:(j - r + 1) * 128],
                                                                          vtt[:, j * 128:(j + 1) * 128], ident[:])
                                   for j in range(r, r + 8)]
                            S.group("pe", fns, reads=vtt_toks + [("ident",)], writes=[pttok])
                            S.op("act", lambda e, r=r, pt=pt: e.copy(vsb[:, r:r + 8, :],
                                                                     pt[:].rearrange("p (a b) -> p a b", b=128)),
                                 reads=[pttok], writes=[("vsb",)])

                    def prefetch(h):
                        for nm, dst, src_ in (("wqb", wqb, d_wq), ("wkb", wkb, d_wk), ("wvb", wvb, d_wv)):
                            S.dma("pool", nm,
                                  lambda e, o=dst, i=src_, h=h: e.dma_start(out=o[:], in_=i[h],
                                                                            max_dma_last_dim=MDL),
                                  writes=[(nm,)])

                    projall(0)
                    vtrans(0)
                    prefetch(1)
                    for h in range(NH):
                        for k in range(NST):
                            if k >= 2:
                                pt_part(k - 2)
                            if k >= 3:
                                pv_part(k - 3, h)
                            s_front(k)
                            if k >= 1:
                                s_back(k - 1)
                        s_back(7)
                        if h + 1 < NH:
                            projall(h + 1)
                            if h + 2 < NH:
                                prefetch(h + 2)
                        pt_part(6)
                        pv_part(5, h)
                        pt_part(7)
                        pv_part(6, h)
                        pv_part(7, h)
                        if h + 1 < NH:
                            vtrans(h + 1)
                    S.flush(nc, semh)

            attn_snap = S.snap()
            wg_b = E(nc.sbuf_tensor("wg_b", [128, NCT, NI], BF16, side="right"))
            wu_b = E(nc.sbuf_tensor("wu_b", [128, NCT, NI], BF16, side="right"))
            with ExitStack() as wp:
                EW = wp.enter_context
                wo_b = [EW(nc.sbuf_tensor(f"wo_b{s}", [128, NCT, 512], BF16)) for s in range(2)]
                bo_bf = EW(nc.sbuf_tensor("bo_bf", [1, C], BF16))
                psW = [EW(nc.psum_tensor(f"psW{m}", [128, 512], F32)) for m in range(NST)]

                S.barrier_at("pool", attn_snap)
                S.dma("pool", "bo", lambda e: e.dma_start(out=bo_bf[:], in_=d_bo[:], max_dma_last_dim=MDL),
                      writes=[("bo",)])
                for co in range(2):
                    S.dma("pool", f"wo{co}",
                          lambda e, o=wo_b[co], co=co: e.dma_start(out=o[:], in_=d_wo[:, :, co * 512:(co + 1) * 512],
                                                                   max_dma_last_dim=MDL),
                          writes=[("wob", co)])
                S.dma("pool", "wg", lambda e: e.dma_start(out=wg_b[:], in_=d_wgt[0], max_dma_last_dim=MDL),
                      writes=[("wg",)])
                S.dma("pool", "wu", lambda e: e.dma_start(out=wu_b[:], in_=d_wut[0], max_dma_last_dim=MDL),
                      writes=[("wu",)])
                for co in range(4):
                    sl = co % 2
                    if co >= 2:
                        S.dma("pool", f"wo{sl}",
                              lambda e, o=wo_b[sl], co=co: e.dma_start(out=o[:], in_=d_wo[:, :, co * 512:(co + 1) * 512],
                                                                       max_dma_last_dim=MDL),
                              writes=[("wob", sl)])
                    for m in range(NST):
                        fns = [lambda e, m=m, ct=ct, sl=sl: e.matmul(psW[m][:], YT[:, ct, m * 128:(m + 1) * 128],
                                                                     wo_b[sl][:, ct, :],
                                                                     start=(ct == 0), stop=False)
                               for ct in range(NCT)]
                        fns.append(lambda e, m=m, co=co: e.matmul(psW[m][:], onr[0:1, :],
                                                                  bo_bf[0:1, co * 512:(co + 1) * 512],
                                                                  start=False, stop=True))
                        S.group("pe", fns, reads=[("yt",), ("wob", sl), ("onr",), ("bo",)],
                                writes=[("psW", m)])
                        S.op("dve", lambda e, m=m, co=co:
                             e.tensor_tensor(x_res[:, m, co * 512:(co + 1) * 512], psW[m][:],
                                             x_res[:, m, co * 512:(co + 1) * 512], ALU.add),
                             reads=[("psW", m), ("xr", m)], writes=[("xr", m)])
                S.flush(nc, semh)

        with ExitStack() as fp:
            EF = fp.enter_context
            X2T = EF(nc.sbuf_tensor("X2T", [128, NCT, TL], BF16))
            wd_b = EF(nc.sbuf_tensor("wd_b", [128, 8, C], BF16))
            g_sb = EF(nc.sbuf_tensor("g_sb", [128, 8, 512], BF16))
            lnb2 = EF(nc.sbuf_tensor("lnb2", [128, 2048], BF16))
            gw_b = EF(nc.sbuf_tensor("gw_b", [128, NCT, NE], BF16))
            ebr_sb = EF(nc.sbuf_tensor("ebr_sb", [1, NE], BF16))
            ebbc = EF(nc.sbuf_tensor("ebbc", [128, NE], F32))
            sgb = EF(nc.sbuf_tensor("sgb", [128, NST, NE], F32))
            c_all = EF(nc.sbuf_tensor("c_all", [128, NST, NE], F32))
            mskA = EF(nc.sbuf_tensor("mskA", [128, NST, NE], F32))
            maccA = EF(nc.sbuf_tensor("maccA", [128, NST, NE], F32))
            wrkA = c_all
            m8a = EF(nc.sbuf_tensor("m8a", [128, NST], F32))
            m8b = EF(nc.sbuf_tensor("m8b", [128, NST], F32))

            S.barrier_queue("pool")
            S.barrier_queue("sp")
            S.dma("pool", "gw", lambda e: e.dma_start(out=gw_b[:], in_=d_gwt[:], max_dma_last_dim=MDL),
                  writes=[("gw",)])
            for hh in range(2):
                S.dma("pool", f"wd{hh}",
                      lambda e, hh=hh: e.dma_start(out=wd_b[:, 4 * hh:4 * (hh + 1), :],
                                                   in_=d_wdt[0][:, 4 * hh:4 * (hh + 1), :],
                                                   max_dma_last_dim=MDL),
                      writes=[("wd", hh)])
            S.dma("pool", "smp", lambda e: e.dma_start(out=ebr_sb[:], in_=d_ebr[:], max_dma_last_dim=MDL),
                  writes=[("ebr",)])
            S.level("smp", [("ebr",)])

            with ExitStack() as rp:
                psL2 = [rp.enter_context(nc.psum_tensor(f"psL2{j}", [128, 1024], BF16)) for j in range(2)]
                psR = rp.enter_context(nc.psum_tensor("psR", [128, NE], F32))

                S.op("pe", lambda e: e.matmul(psR[:], onr[0:1, :], ebr_sb[0:1, :], start=True, stop=True),
                     reads=[("onr",), ("ebr",)], writes=[("psR",)])
                S.op("dve", lambda e: e.tensor_copy(ebbc[:], psR[:]), reads=[("psR",)], writes=[("ebbc",)])

                def router(mt):
                    fns = [lambda e, ct=ct, mt=mt: e.matmul(psR[:], X2T[:, ct, mt * 128:(mt + 1) * 128],
                                                            gw_b[:, ct, :], start=(ct == 0), stop=(ct == NCT - 1))
                           for ct in range(NCT)]
                    S.group("pe", fns, reads=[("x2t", mt, 0), ("x2t", mt, 1), ("gw",)], writes=[("psR",)])
                    S.op("act", lambda e, mt=mt: e.activation(sgb[:, mt, :], psR[:], AF.Sigmoid,
                                                              bias=cst[:, 1:2], scale=1.0),
                         reads=[("psR",), ("cst",)], writes=[("sg", mt)])

                def topk_all():
                    sg_toks = [("sg", mt) for mt in range(NST)]
                    ebb = ebbc[:].rearrange("p b -> p () b").broadcast_to([128, NST, NE])
                    S.op("dve", lambda e, ebb=ebb: e.tensor_tensor(wrkA[:], sgb[:], ebb, ALU.add),
                         reads=sg_toks + [("ebbc",)], writes=[("wrkA",)])
                    S.op("dve", lambda e: e.memset(maccA[:], 0.0), writes=[("maccA",)])
                    for it in range(TOPK):
                        S.op("dve", lambda e: e.reduce_max(m8a[:], wrkA[:], axis=AX.X),
                             reads=[("wrkA",)], writes=[("m8a",)])
                        mb = m8a[:].rearrange("p a -> p a ()").broadcast_to([128, NST, NE])
                        S.op("dve", lambda e, mb=mb: e.tensor_tensor(mskA[:], wrkA[:], mb, ALU.is_ge),
                             reads=[("wrkA",), ("m8a",), ("mskA",)], writes=[("mskA",)])
                        S.op("dve", lambda e: e.tensor_tensor(maccA[:], maccA[:], mskA[:], ALU.add),
                             reads=[("maccA",), ("mskA",)], writes=[("maccA",)])
                        S.op("dve", lambda e: e.scalar_tensor_tensor(wrkA[:], mskA[:], -1.0e30, wrkA[:],
                                                                     ALU.mult, ALU.add),
                             reads=[("mskA",), ("wrkA",)], writes=[("wrkA",)])
                    S.op("dve", lambda e: e.tensor_tensor(mskA[:], sgb[:], maccA[:], ALU.mult),
                         reads=sg_toks + [("maccA",), ("mskA",)], writes=[("mskA",)])
                    S.op("dve", lambda e: e.reduce_sum(m8a[:], mskA[:], axis=AX.X),
                         reads=[("mskA",)], writes=[("m8a",)])
                    S.op("dve", lambda e: e.reciprocal(m8b[:], m8a[:]),
                         reads=[("m8a",)], writes=[("m8b",)])
                    rb2 = m8b[:].rearrange("p a -> p a ()").broadcast_to([128, NST, NE])
                    S.op("dve", lambda e, rb2=rb2: e.tensor_scalar(c_all[:], mskA[:], None, None, ALU.bypass)
                         if False else e.tensor_tensor(c_all[:], mskA[:], rb2, ALU.mult),
                         reads=[("mskA",), ("m8b",)], writes=[("c", mt2) for mt2 in range(NST)])

                for mt in range(NST):
                    sl = mt % 2
                    b = 8 * sl
                    c0, c1, c2 = (("l2", sl, j) for j in range(3))
                    xv = x_res[:, mt, :]
                    bb = 24 * sl
                    S.op("dve", lambda e, x=xv, bb=bb: e.bn_stats(bnr[:, bb:bb + 6], x[:, 0:512]),
                         reads=[("xr", mt)], writes=[("bnr2", sl, 0)])
                    S.op("dve", lambda e, x=xv, bb=bb: e.bn_stats(bnr[:, bb + 6:bb + 12], x[:, 512:1024]),
                         reads=[("xr", mt)], writes=[("bnr2", sl, 1)])
                    S.op("dve", lambda e, x=xv, bb=bb: e.bn_stats(bnr[:, bb + 12:bb + 18], x[:, 1024:1536]),
                         reads=[("xr", mt)], writes=[("bnr2", sl, 2)])
                    S.op("dve", lambda e, x=xv, bb=bb: e.bn_stats(bnr[:, bb + 18:bb + 24], x[:, 1536:2048]),
                         reads=[("xr", mt)], writes=[("bnr2", sl, 3)])
                    S.op("dve", lambda e, bb=bb, b=b: e.bn_aggr(st_ln[:, b:b + 2], bnr[:, bb:bb + 24]),
                         reads=[("bnr2", sl, 0), ("bnr2", sl, 1), ("bnr2", sl, 2), ("bnr2", sl, 3)],
                         writes=[c0])
                    S.op("act", lambda e, b=b: e.activation(st_ln[:, b + 2:b + 3], st_ln[:, b + 1:b + 2],
                                                            AF.Ln, bias=cst[:, 0:1], scale=1.0),
                         reads=[c0, ("cst",)], writes=[c1])
                    S.op("act", lambda e, b=b: e.activation(st_ln[:, b + 3:b + 4], st_ln[:, b + 2:b + 3],
                                                            AF.Exp, bias=cst[:, 1:2], scale=-0.5),
                         reads=[c1, ("cst",)], writes=[c1])
                    S.op("dve", lambda e, b=b: e.scalar_tensor_tensor(st_ln[:, b + 4:b + 5],
                                                                      st_ln[:, b:b + 1], -1.0,
                                                                      st_ln[:, b + 3:b + 4],
                                                                      ALU.mult, ALU.mult),
                         reads=[c0, c1], writes=[c2])
                    S.op("act", lambda e, x=xv, b=b:
                         e.activation(lnb2[:], x, AF.Identity, bias=st_ln[:, b + 4:b + 5],
                                      scale=st_ln[:, b + 3:b + 4]),
                         reads=[("xr", mt), c1, c2, ("lnb2",)], writes=[("lnb2",)])
                    for j in range(2):
                        fns = [lambda e, p=psL2[j], q=q, j=j:
                               e.transpose(p[:, q * 128:(q + 1) * 128],
                                           lnb2[:, (8 * j + q) * 128:(8 * j + q + 1) * 128], ident[:])
                               for q in range(8)]
                        S.group("pe", fns, reads=[("lnb2",), ("ident",)], writes=[("pl", j)])
                        S.op("dve", lambda e, p=psL2[j], mt=mt, j=j:
                             e.tensor_copy(X2T[:, 8 * j:8 * (j + 1), mt * 128:(mt + 1) * 128],
                                           p[:].rearrange("p (a b) -> p a b", b=128)),
                             reads=[("pl", j)], writes=[("x2t", mt, j)])
                    router(mt)
                topk_all()

            with ExitStack() as xp:
                psG = [xp.enter_context(nc.psum_tensor(f"psG{j}", [128, 512], F32)) for j in range(2)]
                psU = [xp.enter_context(nc.psum_tensor(f"psU{j}", [128, 512], F32)) for j in range(2)]
                psD = [xp.enter_context(nc.psum_tensor(f"psD{j}", [128, 512], F32)) for j in range(4)]

                first_grp = [True]
                for e_i in range(NE + 1):
                    shared = (e_i == NE)
                    for th in range(2):
                        for which, wbuf, wtok, ps in (("g", wg_b, ("wg",), psG), ("u", wu_b, ("wu",), psU)):
                            for mi in range(8):
                                p = ps[mi % 2]
                                ptok = (f"ps{which}", mi % 2)
                                fns = [lambda e, p=p, w=wbuf, ct=ct, mi=mi, th=th:
                                       e.matmul(p[:], w[:, ct, mi * 128:(mi + 1) * 128],
                                                X2T[:, ct, th * 512:(th + 1) * 512],
                                                start=(ct == 0), stop=(ct == NCT - 1))
                                       for ct in range(NCT)]
                                wr = [ptok]
                                if which == "u" and first_grp[0]:
                                    wr += [("psR",)]
                                    first_grp[0] = False
                                x2toks = [("x2t", mt2, j2) for mt2 in range(4 * th, 4 * th + 4) for j2 in range(2)]
                                S.group("pe", fns, reads=[wtok] + x2toks, writes=wr)
                                if which == "g":
                                    S.op("act", lambda e, p=p, mi=mi: e.activation(g_sb[:, mi, :], p[:], AF.Silu,
                                                                                   bias=cst[:, 1:2], scale=1.0),
                                         reads=[ptok, ("cst",)], writes=[("gsb", mi)])
                                else:
                                    S.op("dve", lambda e, p=p, mi=mi: e.tensor_tensor(g_sb[:, mi, :], p[:],
                                                                                      g_sb[:, mi, :], ALU.mult),
                                         reads=[ptok, ("gsb", mi)], writes=[("gsb", mi)])
                        if th == 1 and e_i <= NE - 1:
                            nsrc_g = d_swgt if e_i == NE - 1 else d_wgt[e_i + 1]
                            S.dma("pool", "wg", lambda e, s=nsrc_g: e.dma_start(out=wg_b[:], in_=s, max_dma_last_dim=MDL),
                                  writes=[("wg",)])
                        for mtl in range(4):
                            mt = th * 4 + mtl
                            for co in range(4):
                                bnk = (mtl * 4 + co) % 4
                                p = psD[bnk]
                                fns = [lambda e, p=p, ic=ic, mtl=mtl, co=co:
                                       e.matmul(p[:], g_sb[:, ic, mtl * 128:(mtl + 1) * 128],
                                                wd_b[:, ic, co * 512:(co + 1) * 512],
                                                start=(ic == 0), stop=(ic == 7))
                                       for ic in range(8)]
                                S.group("pe", fns,
                                        reads=[("gsb", ic) for ic in range(8)] + [("wd", 0), ("wd", 1)],
                                        writes=[("psD", bnk)])
                                if shared:
                                    S.op("dve", lambda e, p=p, mt=mt, co=co:
                                         e.tensor_tensor(x_res[:, mt, co * 512:(co + 1) * 512], p[:],
                                                         x_res[:, mt, co * 512:(co + 1) * 512], ALU.add),
                                         reads=[("psD", bnk), ("xr", mt)], writes=[("xr", mt)])
                                else:
                                    S.op("dve", lambda e, p=p, mt=mt, co=co, e_i=e_i:
                                         e.scalar_tensor_tensor(x_res[:, mt, co * 512:(co + 1) * 512], p[:],
                                                                c_all[:, mt, e_i:e_i + 1],
                                                                x_res[:, mt, co * 512:(co + 1) * 512],
                                                                ALU.mult, ALU.add),
                                         reads=[("psD", bnk), ("c", mt), ("xr", mt)], writes=[("xr", mt)])
                    if e_i <= NE - 1:
                        nsrc_u = d_swut if e_i == NE - 1 else d_wut[e_i + 1]
                        nsrc_d = d_swdt if e_i == NE - 1 else d_wdt[e_i + 1]
                        S.dma("pool", "wu", lambda e, s=nsrc_u: e.dma_start(out=wu_b[:], in_=s, max_dma_last_dim=MDL),
                              writes=[("wu",)])
                        for hh in range(2):
                            S.dma("pool", f"wd{hh}",
                                  lambda e, s=nsrc_d, hh=hh: e.dma_start(out=wd_b[:, 4 * hh:4 * (hh + 1), :],
                                                                         in_=s[:, 4 * hh:4 * (hh + 1), :],
                                                                         max_dma_last_dim=MDL),
                                  writes=[("wd", hh)])

                for mt in range(NST):
                    S.dma("sp", "out",
                          lambda e, mt=mt: e.dma_start(out=d_out[mt * 128:(mt + 1) * 128, :], in_=x_res[:, mt, :]),
                          reads=[("xr", mt)])
                S.tail_wait("sp", "out")
                S.flush(nc, semh)

    return nc



def core_rows(par):
    k = np.arange(NST)[:, None]
    i = np.arange(128)[None, :]
    return (256 * k + par + 2 * i).reshape(-1)


def build_mask(par):
    i = np.arange(128)[:, None]
    jj2 = np.arange(256)[None, :]
    valid = (jj2 ^ par) <= 2 * i + par
    return np.ascontiguousarray(np.where(valid, 0.0, -1.0e30).astype(np.float32))


def _pack(w):
    return np.ascontiguousarray(w)


_CACHE = {}


def prep_in_maps(L0_attn_x, L0_attn_Wq, L0_attn_bq, L0_attn_Wk, L0_attn_bk,
                 L0_attn_Wv, L0_attn_bv, L0_attn_Wo, L0_attn_bo,
                 L0_ffn_gate_w, L0_ffn_expert_bias, L0_ffn_w_gate, L0_ffn_w_up,
                 L0_ffn_w_down, L0_ffn_sw_gate, L0_ffn_sw_up, L0_ffn_sw_down):
    f = lambda a: np.ascontiguousarray(np.asarray(a, dtype=np.float32))
    x = f(L0_attn_x)
    wq, wk, wv = f(L0_attn_Wq), f(L0_attn_Wk), f(L0_attn_Wv)
    bq, bk, bv = f(L0_attn_bq), f(L0_attn_bk), f(L0_attn_bv)
    wo, bo = f(L0_attn_Wo), f(L0_attn_bo)
    gw, eb = f(L0_ffn_gate_w), f(L0_ffn_expert_bias)
    wg, wu, wd = f(L0_ffn_w_gate), f(L0_ffn_w_up), f(L0_ffn_w_down)
    swg, swu, swd = f(L0_ffn_sw_gate), f(L0_ffn_sw_up), f(L0_ffn_sw_down)

    def chunk2(a):
        return np.ascontiguousarray(a.reshape(a.shape[0] // 128, 128, -1).transpose(1, 0, 2))

    def chunk3(a):
        e, c, xx = a.shape
        return np.ascontiguousarray(a.reshape(e, c // 128, 128, xx).transpose(0, 2, 1, 3))

    common = {
        "wq": chunk3(wq), "wk": chunk3(wk), "wv": chunk3(wv),
        "bqt": np.ascontiguousarray(bq.T), "bkt": np.ascontiguousarray(bk.T),
        "bvt": np.ascontiguousarray(bv.T),
        "wo": chunk2(wo), "bo": np.ascontiguousarray(bo[None, :]),
        "gwt": chunk2(np.ascontiguousarray(gw.T)),
        "ebr": np.ascontiguousarray(eb[None, :]),
        "wgt": chunk3(np.ascontiguousarray(wg.transpose(0, 2, 1))),
        "wut": chunk3(np.ascontiguousarray(wu.transpose(0, 2, 1))),
        "wdt": chunk3(np.ascontiguousarray(wd.transpose(0, 2, 1))),
        "swgt": chunk2(np.ascontiguousarray(swg.T)),
        "swut": chunk2(np.ascontiguousarray(swu.T)),
        "swdt": chunk2(np.ascontiguousarray(swd.T)),
    }
    masks = {p: build_mask(p) for p in (0, 1)}
    perms = {p: (np.arange(2048) ^ p) for p in (0, 1)}

    in_maps = []
    for c in range(8):
        b, par = c // 2, c % 2
        m = dict(common)
        m["xkv"] = np.ascontiguousarray(x[b][perms[par]])
        m["mask"] = masks[par]
        in_maps.append(m)
    return in_maps


def kernel(**inputs):
    in_maps = prep_in_maps(**inputs)
    if "nc" not in _CACHE:
        _CACHE["nc"] = build_program()
    nc = _CACHE["nc"]
    res = run_bass_kernel_spmd(nc, in_maps, core_ids=list(range(8)))

    out = np.empty((4, 2048, C), np.float32)
    rows = {p: core_rows(p) for p in (0, 1)}
    for c in range(8):
        b, par = c // 2, c % 2
        out[b, rows[par]] = res.results[c]["out"]
    return out


# version 15
# speedup vs baseline: 1.0928x; 1.0928x over previous
import numpy as np
from contextlib import ExitStack

import concourse.bass as bass
import concourse.mybir as mybir
from concourse.bass_utils import run_bass_kernel_spmd

F32 = mybir.dt.float32
BF16 = mybir.dt.bfloat16
I32 = mybir.dt.int32
AF = mybir.ActivationFunctionType
ALU = mybir.AluOpType
AX = mybir.AxisListType

C = 2048
NCT = 16
NH = 16
HD = 128
NST = 8
TL = 1024
NE = 16
NI = 1024
TOPK = 4
EPS = 1e-5
SCALE = float(HD) ** -0.5
MDL = 8192

ENG = ("pe", "dve", "act", "pool", "sp")


class Sched:
    def __init__(self):
        self.items = {n: [] for n in ENG}
        self.waited = {n: {} for n in ENG}
        self.cnt = {}
        self.last = {}
        self.readers = {}
        self.dma_sems = set()

    def _need(self, reads, writes):
        need = {}

        def add(sem, val):
            if val > need.get(sem, 0):
                need[sem] = val

        for t in reads:
            lw = self.last.get(t)
            if lw:
                add(*lw)
        for t in writes:
            lw = self.last.get(t)
            if lw:
                add(*lw)
            for sem, val in self.readers.get(t, {}).items():
                add(sem, val)
        return need

    def _waits(self, st, need):
        for sem in sorted(need):
            val = need[sem]
            if self.waited[st].get(sem, 0) < val:
                self.items[st].append(("w", sem, val))
                self.waited[st][sem] = val

    def group(self, st, fns, reads=(), writes=()):
        assert st in ("pe", "dve", "act", "pool")
        self._waits(st, self._need(reads, writes))
        self.cnt[st] = self.cnt.get(st, 0) + 1
        v = self.cnt[st]
        for f in fns[:-1]:
            self.items[st].append(("op", f, None, 0))
        self.items[st].append(("op", fns[-1], st, 1))
        for t in writes:
            self.last[t] = (st, v)
            self.readers[t] = {}
        for t in reads:
            self.readers.setdefault(t, {})[st] = v

    def op(self, st, fn, reads=(), writes=()):
        self.group(st, [fn], reads, writes)

    def dma(self, q, sem, fn, reads=(), writes=()):
        assert q in ("sp", "pool")
        self.dma_sems.add(sem)
        self._waits(q, self._need(reads, writes))
        self.cnt[sem] = self.cnt.get(sem, 0) + 16
        v = self.cnt[sem]
        self.items[q].append(("op", fn, sem, 16))
        for t in writes:
            self.last[t] = (sem, v)
            self.readers[t] = {}
        for t in reads:
            self.readers.setdefault(t, {})[sem] = v

    def level(self, sem, tokens):
        for t in tokens:
            if t in self.last:
                self.last[t] = (sem, self.cnt.get(sem, 0))

    def snap(self):
        return {s: self.cnt.get(s, 0) for s in ("pe", "dve", "act", "pool")}

    def barrier_at(self, q, snap):
        self._waits(q, {s: v for s, v in snap.items() if v})

    def barrier_queue(self, q):
        need = {}
        for s in ("pe", "dve", "act", "pool"):
            if self.cnt.get(s, 0):
                need[s] = self.cnt[s]
        for s in self.dma_sems:
            if self.cnt.get(s, 0):
                need[s] = self.cnt[s]
        self._waits(q, need)

    def tail_wait(self, q, sem):
        self.waited[q].pop(sem, None)
        self._waits(q, {sem: self.cnt.get(sem, 0)})

    def flush(self, nc, semh):
        if not any(self.items[n] for n in ENG):
            return
        with nc.Block() as blk:
            for st, attr in (("pe", "tensor"), ("dve", "vector"),
                             ("act", "scalar"), ("pool", "gpsimd"),
                             ("sp", "sync")):
                items = self.items[st]
                self.items[st] = []
                if not items:
                    continue

                def mk(items=items):
                    def f(eng):
                        for it in items:
                            if it[0] == "w":
                                eng.wait_ge(semh[it[1]], it[2])
                            else:
                                inst = it[1](eng)
                                if it[2] is not None:
                                    inst.then_inc(semh[it[2]], it[3])
                    return f

                getattr(blk, attr)(mk())


ALL_SEMS = (["pe", "dve", "act", "pool", "sm", "smp", "xr", "gw", "bo",
             "xkv0", "xkv1", "xkv2", "wqb", "wkb", "wvb", "wo0", "wo1",
             "wg", "wu", "wd0", "wd1", "out"])


def build_program():
    nc = bass.Bass("TRN2", target_bir_lowering=False)

    d_xkv = nc.dram_tensor("xkv", [2048, C], F32, kind="ExternalInput").ap()
    d_mask = nc.dram_tensor("mask", [128, 256], F32, kind="ExternalInput").ap()
    d_wq = nc.dram_tensor("wq", [NH, 128, NCT, HD], F32, kind="ExternalInput").ap()
    d_wk = nc.dram_tensor("wk", [NH, 128, NCT, HD], F32, kind="ExternalInput").ap()
    d_wv = nc.dram_tensor("wv", [NH, 128, NCT, HD], F32, kind="ExternalInput").ap()
    d_bqt = nc.dram_tensor("bqt", [HD, NH], F32, kind="ExternalInput").ap()
    d_bkt = nc.dram_tensor("bkt", [HD, NH], F32, kind="ExternalInput").ap()
    d_bvt = nc.dram_tensor("bvt", [HD, NH], F32, kind="ExternalInput").ap()
    d_wo = nc.dram_tensor("wo", [128, NCT, C], F32, kind="ExternalInput").ap()
    d_bo = nc.dram_tensor("bo", [1, C], F32, kind="ExternalInput").ap()
    d_gwt = nc.dram_tensor("gwt", [128, NCT, NE], F32, kind="ExternalInput").ap()
    d_ebr = nc.dram_tensor("ebr", [1, NE], F32, kind="ExternalInput").ap()
    d_wgt = nc.dram_tensor("wgt", [NE, 128, NCT, NI], F32, kind="ExternalInput").ap()
    d_wut = nc.dram_tensor("wut", [NE, 128, NCT, NI], F32, kind="ExternalInput").ap()
    d_wdt = nc.dram_tensor("wdt", [NE, 128, 8, C], F32, kind="ExternalInput").ap()
    d_swgt = nc.dram_tensor("swgt", [128, NCT, NI], F32, kind="ExternalInput").ap()
    d_swut = nc.dram_tensor("swut", [128, NCT, NI], F32, kind="ExternalInput").ap()
    d_swdt = nc.dram_tensor("swdt", [128, 8, C], F32, kind="ExternalInput").ap()
    d_out = nc.dram_tensor("out", [TL, C], F32, kind="ExternalOutput").ap()

    S = Sched()

    with ExitStack() as top:
        semh = {n: top.enter_context(nc.semaphore(f"s_{n}")) for n in ALL_SEMS}
        E = top.enter_context

        x_res = E(nc.sbuf_tensor("x_res", [128, NST, C], F32))
        ident = E(nc.sbuf_tensor("ident", [128, 128], BF16))
        cst = E(nc.sbuf_tensor("cst", [128, 3], F32))
        onr = E(nc.sbuf_tensor("onr", [1, 128], BF16))
        st_ln = E(nc.sbuf_tensor("st_ln", [128, 16], F32))
        st_at = E(nc.sbuf_tensor("st_at", [128, 24], F32))
        bnr = E(nc.sbuf_tensor("bnr", [128, 48], F32))

        with ExitStack() as sAB:
            EB = sAB.enter_context
            YT = EB(nc.sbuf_tensor("YT", [128, NH, TL], BF16))

            with ExitStack() as s1:
                E1 = s1.enter_context
                ln1T = E1(nc.sbuf_tensor("ln1T", [128, NCT, 2048], BF16))
                wqb = E1(nc.sbuf_tensor("wqb", [128, NCT, HD], BF16))
                wkb = E1(nc.sbuf_tensor("wkb", [128, NCT, HD], BF16))
                wvb = E1(nc.sbuf_tensor("wvb", [128, NCT, HD], BF16))
                mask_sb = E1(nc.sbuf_tensor("mask_sb", [128, 256], BF16))
                bqt_s = E1(nc.sbuf_tensor("bqt_s", [HD, NH], F32))
                bkt_sb = E1(nc.sbuf_tensor("bkt_sb", [HD, NH], F32))
                bvt_sb = E1(nc.sbuf_tensor("bvt_sb", [HD, NH], F32))

                with ExitStack() as p1:
                    EP = p1.enter_context
                    io32 = EP(nc.sbuf_tensor("io32", [128, 128], I32))
                    bqt_r = EP(nc.sbuf_tensor("bqt_r", [HD, NH], F32))
                    xsl = [EP(nc.sbuf_tensor(f"xsl{s}", [128, 2048], F32)) for s in range(3)]
                    lnb = [EP(nc.sbuf_tensor(f"lnb{s}", [128, 2048], BF16)) for s in range(2)]
                    psLN = [EP(nc.psum_tensor(f"psLN{j}", [128, 1024], BF16)) for j in range(4)]

                    S.dma("pool", "smp", lambda e, o=mask_sb, i=d_mask: e.dma_start(out=o[:], in_=i[:]), writes=[("mask",)])
                    S.dma("sp", "sm", lambda e, o=bqt_r, i=d_bqt: e.dma_start(out=o[:], in_=i[:]), writes=[("bqtr",)])
                    S.dma("sp", "sm", lambda e, o=bkt_sb, i=d_bkt: e.dma_start(out=o[:], in_=i[:]), writes=[("bkt",)])
                    S.dma("sp", "sm", lambda e, o=bvt_sb, i=d_bvt: e.dma_start(out=o[:], in_=i[:]), writes=[("bvt",)])
                    S.level("sm", [("bqtr",), ("bkt",), ("bvt",)])
                    S.level("smp", [("mask",)])
                    S.op("pool", lambda e, a=io32: e.iota(a[:], [[1, 128]], base=0, channel_multiplier=-1),
                         writes=[("io32",)])
                    S.op("dve", lambda e, a=ident, b=io32: e.tensor_scalar(a[:], b[:], 0.0, None, ALU.is_equal),
                         reads=[("io32",)], writes=[("ident",)])
                    S.op("dve", lambda e, a=cst: e.memset(a[:, 0:1], EPS), writes=[("cst",)])
                    S.op("dve", lambda e, a=cst: e.memset(a[:, 1:2], 0.0), reads=[("cst",)], writes=[("cst",)])
                    S.op("dve", lambda e, a=cst: e.memset(a[:, 2:3], -20.0), reads=[("cst",)], writes=[("cst",)])
                    S.op("dve", lambda e, a=onr: e.memset(a[:], 1.0), writes=[("onr",)])
                    S.op("dve", lambda e, a=bqt_s, b=bqt_r: e.tensor_scalar(a[:], b[:], SCALE, None, ALU.mult),
                         reads=[("bqtr",)], writes=[("bqts",)])

                    for nm, dst, src in (("wqb", wqb, d_wq), ("wkb", wkb, d_wk), ("wvb", wvb, d_wv)):
                        S.dma("pool", nm,
                              lambda e, o=dst, i=src: e.dma_start(out=o[:], in_=i[0], max_dma_last_dim=MDL),
                              writes=[(nm,)])

                    def ln1_load(i):
                        s3 = i % 3
                        S.dma("sp", f"xkv{s3}",
                              lambda e, o=xsl[s3], i2=d_xkv, i=i: e.dma_start(out=o[:], in_=i2[i * 128:(i + 1) * 128, :]),
                              writes=[("xsl", s3)])

                    def ln1_stats(i):
                        sl = i % 2
                        s3 = i % 3
                        b = 8 * sl
                        bb = 24 * sl
                        for cc in range(4):
                            S.op("dve", lambda e, x=xsl[s3], bb=bb, cc=cc:
                                 e.bn_stats(bnr[:, bb + 6 * cc:bb + 6 * cc + 6], x[:, 512 * cc:512 * (cc + 1)]),
                                 reads=[("xsl", s3)], writes=[("bnr", sl, cc)])
                        S.op("dve", lambda e, bb=bb, b=b: e.bn_aggr(st_ln[:, b:b + 2], bnr[:, bb:bb + 24]),
                             reads=[("bnr", sl, 0), ("bnr", sl, 1), ("bnr", sl, 2), ("bnr", sl, 3)],
                             writes=[("lst", sl, 0)])
                        S.op("act", lambda e, b=b: e.activation(st_ln[:, b + 2:b + 3], st_ln[:, b + 1:b + 2],
                                                                AF.Ln, bias=cst[:, 0:1], scale=1.0),
                             reads=[("lst", sl, 0), ("cst",)], writes=[("lst", sl, 1)])
                        S.op("act", lambda e, b=b: e.activation(st_ln[:, b + 3:b + 4], st_ln[:, b + 2:b + 3],
                                                                AF.Exp, bias=cst[:, 1:2], scale=-0.5),
                             reads=[("lst", sl, 1), ("cst",)], writes=[("lst", sl, 1)])

                    def ln1_norm(i):
                        sl = i % 2
                        s3 = i % 3
                        b = 8 * sl
                        S.op("dve", lambda e, b=b: e.scalar_tensor_tensor(st_ln[:, b + 4:b + 5],
                                                                          st_ln[:, b:b + 1], -1.0,
                                                                          st_ln[:, b + 3:b + 4],
                                                                          ALU.mult, ALU.mult),
                             reads=[("lst", sl, 0), ("lst", sl, 1)], writes=[("lst", sl, 2)])
                        S.op("act", lambda e, o=lnb[sl], x=xsl[s3], b=b:
                             e.activation(o[:], x[:], AF.Identity, bias=st_ln[:, b + 4:b + 5],
                                          scale=st_ln[:, b + 3:b + 4]),
                             reads=[("xsl", s3), ("lst", sl, 1), ("lst", sl, 2)], writes=[("lnb", sl)])

                    def ln1_tp(i):
                        sl = i % 2
                        for j in range(2):
                            bj = 2 * (i % 2) + j
                            fns = []
                            for q in range(8):
                                fns.append(lambda e, p=psLN[bj], src=lnb[sl], q=q, j=j:
                                           e.transpose(p[:, q * 128:(q + 1) * 128],
                                                       src[:, (8 * j + q) * 128:(8 * j + q + 1) * 128], ident[:]))
                            S.group("pe", fns, reads=[("lnb", sl), ("ident",)], writes=[("psLN", bj)])

                    def ln1_ev(i):
                        for j in range(2):
                            bj = 2 * (i % 2) + j
                            S.op("dve", lambda e, p=psLN[bj], i=i, j=j:
                                 e.tensor_copy(ln1T[:, 8 * j:8 * (j + 1), i * 128:(i + 1) * 128],
                                               p[:].rearrange("p (a b) -> p a b", b=128)),
                                 reads=[("psLN", bj)], writes=[("lnT",)])

                    ln1_load(0)
                    ln1_load(1)
                    for i in range(16):
                        ln1_stats(i)
                        if i >= 1:
                            ln1_norm(i - 1)
                            ln1_tp(i - 1)
                            if i + 1 < 16:
                                ln1_load(i + 1)
                        if i >= 2:
                            ln1_ev(i - 2)
                    ln1_norm(15)
                    ln1_tp(15)
                    ln1_ev(14)
                    ln1_ev(15)

                    for k in range(NST):
                        S.dma("sp", "xr",
                              lambda e, o=x_res, i=d_xkv, k=k:
                              e.dma_start(out=o[:, k, :], in_=i[256 * k:256 * (k + 1):2, :]),
                              writes=[("xr", k)])
                    S.level("xr", [("xr", k) for k in range(NST)])



                    S.flush(nc, semh)

                with ExitStack() as at:
                    EA = at.enter_context
                    qt = EA(nc.sbuf_tensor("qt", [128, TL], BF16))
                    kt = EA(nc.sbuf_tensor("kt", [128, 2048], BF16))
                    vtt = EA(nc.sbuf_tensor("vtt", [128, 2048], BF16))
                    vsb = EA(nc.sbuf_tensor("vsb", [128, NCT, HD], BF16))
                    psb = [EA(nc.sbuf_tensor(f"psb{s}", [128, 2048], BF16)) for s in range(3)]
                    ptb = [EA(nc.sbuf_tensor(f"ptb{s}", [128, NCT, HD], BF16)) for s in range(2)]
                    psP = [EA(nc.psum_tensor(f"psP{j}", [128, 512], F32)) for j in range(2)]
                    psS = EA(nc.psum_tensor("psS", [128, 1536], F32))
                    psPT = [EA(nc.psum_tensor(f"psPT{j}", [128, 1024], BF16)) for j in range(2)]
                    psPV = EA(nc.psum_tensor("psPV", [128, 512], F32))

                    lnr = ln1T[:].rearrange("p c (k w) -> p c k w", w=256)
                    rb = [0]

                    def proj(h, wbuf, wtok, bias_ap, btok, scal, dst, dtok, nchunks, pp, stride2):
                        for ncx in range(nchunks):
                            p = psP[pp[0] % 2]
                            ptok = ("psP", pp[0] % 2)
                            pp[0] += 1
                            fns = []
                            for ct in range(NCT):
                                if stride2:
                                    rhs = lnr[:, ct, 4 * ncx:4 * ncx + 4, 0:256:2]
                                else:
                                    rhs = ln1T[:, ct, ncx * 512:(ncx + 1) * 512]
                                fns.append(lambda e, p=p, w=wbuf, ct=ct, rhs=rhs:
                                           e.matmul(p[:], w[:, ct, :], rhs,
                                                    start=(ct == 0), stop=(ct == NCT - 1)))
                            S.group("pe", fns, reads=[(wtok,), ("lnT",)], writes=[ptok])
                            S.op("act", lambda e, p=p, d=dst, ncx=ncx, b=bias_ap, sc=scal:
                                 e.activation(d[:, ncx * 512:(ncx + 1) * 512], p[:], AF.Identity,
                                              bias=b, scale=sc),
                                 reads=[ptok, btok], writes=[dtok + (ncx,)])

                    def s_front(k):
                        par = k % 2
                        pb = k % 3
                        base = 12 * par
                        L = 256 * (k + 1)
                        widths = [512] * (L // 512) + ([256] if L % 512 else [])
                        nchunks = len(widths)
                        off = 0
                        for ci, w in enumerate(widths):
                            bk = rb[0] % 3
                            rb[0] += 1
                            S.group("pe",
                                    [lambda e, bk=bk, w=w, off=off, k=k:
                                     e.matmul(psS[:, bk * 512:bk * 512 + w],
                                              qt[:, k * 128:(k + 1) * 128],
                                              kt[:, off:off + w], start=True, stop=True)],
                                    reads=[("qt", k // 4), ("kt", off // 512)], writes=[("psS", bk)])
                            if ci == nchunks - 1:
                                mo = bk * 512 + w - 256
                                S.op("dve", lambda e, mo=mo, bk=bk:
                                     e.scalar_tensor_tensor(psS[:, mo:mo + 256], psS[:, mo:mo + 256],
                                                            90.0, mask_sb[:], ALU.min, ALU.add),
                                     reads=[("psS", bk), ("mask",)], writes=[("psS", bk)])
                            S.op("act", lambda e, bk=bk, w=w, off=off, base=base, ci=ci, pb=pb:
                                 e.activation(psb[pb][:, off:off + w], psS[:, bk * 512:bk * 512 + w], AF.Exp,
                                              bias=cst[:, 2:3], scale=1.0,
                                              accum_out=st_at[:, base + 5 + ci:base + 6 + ci]),
                                 reads=[("psS", bk), ("cst",)], writes=[("psb", pb), ("sum", par, ci)])
                            off += w

                    def s_back(k):
                        par = k % 2
                        pb = k % 3
                        base = 12 * par
                        L = 256 * (k + 1)
                        nchunks = len([512] * (L // 512) + ([256] if L % 512 else []))
                        S.op("dve", lambda e, base=base, nchunks=nchunks:
                             e.reduce_sum(st_at[:, base + 9:base + 10], st_at[:, base + 5:base + 5 + nchunks],
                                          axis=AX.X),
                             reads=[("sum", par, ci) for ci in range(nchunks)], writes=[("tsum", par)])
                        S.op("dve", lambda e, base=base: e.reciprocal(st_at[:, base + 10:base + 11],
                                                                     st_at[:, base + 9:base + 10]),
                             reads=[("tsum", par)], writes=[("at2", par)])
                        S.op("act", lambda e, L=L, base=base, pb=pb:
                             e.activation(psb[pb][:, 0:L], psb[pb][:, 0:L], AF.Identity,
                                          bias=cst[:, 1:2], scale=st_at[:, base + 10:base + 11]),
                             reads=[("at2", par), ("psb", pb), ("cst",)], writes=[("psb", pb)])

                    def pt_part(k):
                        pb = k % 3
                        tb = k % 2
                        nkc = 2 * (k + 1)
                        for ri, r in enumerate(range(0, nkc, 8)):
                            pt = psPT[ri % 2]
                            pttok = ("psPT", ri % 2)
                            n = min(8, nkc - r)
                            fns = [lambda e, j=j, r=r, pb=pb, pt=pt:
                                   e.transpose(pt[:, (j - r) * 128:(j - r + 1) * 128],
                                               psb[pb][:, j * 128:(j + 1) * 128], ident[:])
                                   for j in range(r, r + n)]
                            S.group("pe", fns, reads=[("psb", pb), ("ident",)], writes=[pttok])
                            S.op("act", lambda e, r=r, n=n, tb=tb, pt=pt:
                                 e.copy(ptb[tb][:, r:r + n, :],
                                        pt[:, 0:n * 128].rearrange("p (a b) -> p a b", b=128)),
                                 reads=[pttok], writes=[("ptb", tb)])

                    def pv_part(k, h):
                        tb = k % 2
                        nkc = 2 * (k + 1)
                        fns = [lambda e, j=j, k=k, nkc=nkc, tb=tb:
                               e.matmul(psPV[:, (k % 4) * 128:(k % 4 + 1) * 128],
                                        vsb[:, j, :], ptb[tb][:, j, :],
                                        start=(j == 0), stop=(j == nkc - 1))
                               for j in range(nkc)]
                        S.group("pe", fns, reads=[("vsb",), ("ptb", tb)], writes=[("psPV",)])
                        if k % 4 == 3:
                            S.op("act", lambda e, h=h, k=k:
                                 e.copy(YT[:, h, (k - 3) * 128:(k + 1) * 128], psPV[:]),
                                 reads=[("psPV",)], writes=[("yt",)])

                    vtt_toks = [("vtt", j) for j in range(4)]

                    def projall(h):
                        pp = [0]
                        proj(h, wqb, "wqb", bqt_s[:, h:h + 1], ("bqts",), SCALE, qt, ("qt",), 2, pp, True)
                        proj(h, wkb, "wkb", bkt_sb[:, h:h + 1], ("bkt",), 1.0, kt, ("kt",), 4, pp, False)
                        proj(h, wvb, "wvb", bvt_sb[:, h:h + 1], ("bvt",), 1.0, vtt, ("vtt",), 4, pp, False)

                    def vtrans(h):
                        for ri, r in enumerate(range(0, 16, 8)):
                            pt = psPT[ri % 2]
                            pttok = ("psPT", ri % 2)
                            fns = [lambda e, j=j, r=r, pt=pt: e.transpose(pt[:, (j - r) * 128:(j - r + 1) * 128],
                                                                          vtt[:, j * 128:(j + 1) * 128], ident[:])
                                   for j in range(r, r + 8)]
                            S.group("pe", fns, reads=vtt_toks + [("ident",)], writes=[pttok])
                            S.op("act", lambda e, r=r, pt=pt: e.copy(vsb[:, r:r + 8, :],
                                                                     pt[:].rearrange("p (a b) -> p a b", b=128)),
                                 reads=[pttok], writes=[("vsb",)])

                    def prefetch(h):
                        for nm, dst, src_ in (("wqb", wqb, d_wq), ("wkb", wkb, d_wk), ("wvb", wvb, d_wv)):
                            S.dma("pool", nm,
                                  lambda e, o=dst, i=src_, h=h: e.dma_start(out=o[:], in_=i[h],
                                                                            max_dma_last_dim=MDL),
                                  writes=[(nm,)])

                    projall(0)
                    vtrans(0)
                    prefetch(1)
                    for h in range(NH):
                        for k in range(NST):
                            s_front(k)
                            if k >= 1:
                                s_back(k - 1)
                            if k >= 2:
                                pt_part(k - 2)
                            if k >= 3:
                                pv_part(k - 3, h)
                        s_back(7)
                        if h + 1 < NH:
                            projall(h + 1)
                            if h + 2 < NH:
                                prefetch(h + 2)
                        pt_part(6)
                        pv_part(5, h)
                        pt_part(7)
                        pv_part(6, h)
                        pv_part(7, h)
                        if h + 1 < NH:
                            vtrans(h + 1)
                    S.flush(nc, semh)

            attn_snap = S.snap()
            wg_b = E(nc.sbuf_tensor("wg_b", [128, NCT, NI], BF16, side="right"))
            wu_b = E(nc.sbuf_tensor("wu_b", [128, NCT, NI], BF16, side="right"))
            with ExitStack() as wp:
                EW = wp.enter_context
                wo_b = [EW(nc.sbuf_tensor(f"wo_b{s}", [128, NCT, 512], BF16)) for s in range(2)]
                bo_bf = EW(nc.sbuf_tensor("bo_bf", [1, C], BF16))
                psW = [EW(nc.psum_tensor(f"psW{m}", [128, 512], F32)) for m in range(NST)]

                S.barrier_at("pool", attn_snap)
                S.dma("pool", "bo", lambda e: e.dma_start(out=bo_bf[:], in_=d_bo[:], max_dma_last_dim=MDL),
                      writes=[("bo",)])
                for co in range(2):
                    S.dma("pool", f"wo{co}",
                          lambda e, o=wo_b[co], co=co: e.dma_start(out=o[:], in_=d_wo[:, :, co * 512:(co + 1) * 512],
                                                                   max_dma_last_dim=MDL),
                          writes=[("wob", co)])
                S.dma("pool", "wg", lambda e: e.dma_start(out=wg_b[:], in_=d_wgt[0], max_dma_last_dim=MDL),
                      writes=[("wg",)])
                S.dma("pool", "wu", lambda e: e.dma_start(out=wu_b[:], in_=d_wut[0], max_dma_last_dim=MDL),
                      writes=[("wu",)])
                for co in range(4):
                    sl = co % 2
                    if co >= 2:
                        S.dma("pool", f"wo{sl}",
                              lambda e, o=wo_b[sl], co=co: e.dma_start(out=o[:], in_=d_wo[:, :, co * 512:(co + 1) * 512],
                                                                       max_dma_last_dim=MDL),
                              writes=[("wob", sl)])
                    for m in range(NST):
                        fns = [lambda e, m=m, ct=ct, sl=sl: e.matmul(psW[m][:], YT[:, ct, m * 128:(m + 1) * 128],
                                                                     wo_b[sl][:, ct, :],
                                                                     start=(ct == 0), stop=False)
                               for ct in range(NCT)]
                        fns.append(lambda e, m=m, co=co: e.matmul(psW[m][:], onr[0:1, :],
                                                                  bo_bf[0:1, co * 512:(co + 1) * 512],
                                                                  start=False, stop=True))
                        S.group("pe", fns, reads=[("yt",), ("wob", sl), ("onr",), ("bo",)],
                                writes=[("psW", m)])
                        S.op("dve", lambda e, m=m, co=co:
                             e.tensor_tensor(x_res[:, m, co * 512:(co + 1) * 512], psW[m][:],
                                             x_res[:, m, co * 512:(co + 1) * 512], ALU.add),
                             reads=[("psW", m), ("xr", m)], writes=[("xr", m)])
                S.flush(nc, semh)

        with ExitStack() as fp:
            EF = fp.enter_context
            X2T = EF(nc.sbuf_tensor("X2T", [128, NCT, TL], BF16))
            wd_b = EF(nc.sbuf_tensor("wd_b", [128, 8, C], BF16))
            g_sb = EF(nc.sbuf_tensor("g_sb", [128, 8, 512], BF16))
            lnb2 = EF(nc.sbuf_tensor("lnb2", [128, 2048], BF16))
            gw_b = EF(nc.sbuf_tensor("gw_b", [128, NCT, NE], BF16))
            ebr_sb = EF(nc.sbuf_tensor("ebr_sb", [1, NE], BF16))
            ebbc = EF(nc.sbuf_tensor("ebbc", [128, NE], F32))
            sgb = EF(nc.sbuf_tensor("sgb", [128, NST, NE], F32))
            c_all = EF(nc.sbuf_tensor("c_all", [128, NST, NE], F32))
            mskA = EF(nc.sbuf_tensor("mskA", [128, NST, NE], F32))
            maccA = EF(nc.sbuf_tensor("maccA", [128, NST, NE], F32))
            wrkA = c_all
            m8a = EF(nc.sbuf_tensor("m8a", [128, NST], F32))
            m8b = EF(nc.sbuf_tensor("m8b", [128, NST], F32))

            S.barrier_queue("pool")
            S.barrier_queue("sp")
            S.dma("pool", "gw", lambda e: e.dma_start(out=gw_b[:], in_=d_gwt[:], max_dma_last_dim=MDL),
                  writes=[("gw",)])
            for hh in range(2):
                S.dma("pool", f"wd{hh}",
                      lambda e, hh=hh: e.dma_start(out=wd_b[:, 4 * hh:4 * (hh + 1), :],
                                                   in_=d_wdt[0][:, 4 * hh:4 * (hh + 1), :],
                                                   max_dma_last_dim=MDL),
                      writes=[("wd", hh)])
            S.dma("pool", "smp", lambda e: e.dma_start(out=ebr_sb[:], in_=d_ebr[:], max_dma_last_dim=MDL),
                  writes=[("ebr",)])
            S.level("smp", [("ebr",)])

            with ExitStack() as rp:
                psL2 = [rp.enter_context(nc.psum_tensor(f"psL2{j}", [128, 1024], BF16)) for j in range(2)]
                psR = rp.enter_context(nc.psum_tensor("psR", [128, NE], F32))

                S.op("pe", lambda e: e.matmul(psR[:], onr[0:1, :], ebr_sb[0:1, :], start=True, stop=True),
                     reads=[("onr",), ("ebr",)], writes=[("psR",)])
                S.op("dve", lambda e: e.tensor_copy(ebbc[:], psR[:]), reads=[("psR",)], writes=[("ebbc",)])

                def router(mt):
                    fns = [lambda e, ct=ct, mt=mt: e.matmul(psR[:], X2T[:, ct, mt * 128:(mt + 1) * 128],
                                                            gw_b[:, ct, :], start=(ct == 0), stop=(ct == NCT - 1))
                           for ct in range(NCT)]
                    S.group("pe", fns, reads=[("x2t", mt, 0), ("x2t", mt, 1), ("gw",)], writes=[("psR",)])
                    S.op("act", lambda e, mt=mt: e.activation(sgb[:, mt, :], psR[:], AF.Sigmoid,
                                                              bias=cst[:, 1:2], scale=1.0),
                         reads=[("psR",), ("cst",)], writes=[("sg", mt)])

                def topk_all():
                    sg_toks = [("sg", mt) for mt in range(NST)]
                    ebb = ebbc[:].rearrange("p b -> p () b").broadcast_to([128, NST, NE])
                    S.op("dve", lambda e, ebb=ebb: e.tensor_tensor(wrkA[:], sgb[:], ebb, ALU.add),
                         reads=sg_toks + [("ebbc",)], writes=[("wrkA",)])
                    S.op("dve", lambda e: e.memset(maccA[:], 0.0), writes=[("maccA",)])
                    for it in range(TOPK):
                        S.op("dve", lambda e: e.reduce_max(m8a[:], wrkA[:], axis=AX.X),
                             reads=[("wrkA",)], writes=[("m8a",)])
                        mb = m8a[:].rearrange("p a -> p a ()").broadcast_to([128, NST, NE])
                        S.op("dve", lambda e, mb=mb: e.tensor_tensor(mskA[:], wrkA[:], mb, ALU.is_ge),
                             reads=[("wrkA",), ("m8a",), ("mskA",)], writes=[("mskA",)])
                        S.op("dve", lambda e: e.tensor_tensor(maccA[:], maccA[:], mskA[:], ALU.add),
                             reads=[("maccA",), ("mskA",)], writes=[("maccA",)])
                        S.op("dve", lambda e: e.scalar_tensor_tensor(wrkA[:], mskA[:], -1.0e30, wrkA[:],
                                                                     ALU.mult, ALU.add),
                             reads=[("mskA",), ("wrkA",)], writes=[("wrkA",)])
                    S.op("dve", lambda e: e.tensor_tensor(mskA[:], sgb[:], maccA[:], ALU.mult),
                         reads=sg_toks + [("maccA",), ("mskA",)], writes=[("mskA",)])
                    S.op("dve", lambda e: e.reduce_sum(m8a[:], mskA[:], axis=AX.X),
                         reads=[("mskA",)], writes=[("m8a",)])
                    S.op("dve", lambda e: e.reciprocal(m8b[:], m8a[:]),
                         reads=[("m8a",)], writes=[("m8b",)])
                    rb2 = m8b[:].rearrange("p a -> p a ()").broadcast_to([128, NST, NE])
                    S.op("dve", lambda e, rb2=rb2: e.tensor_scalar(c_all[:], mskA[:], None, None, ALU.bypass)
                         if False else e.tensor_tensor(c_all[:], mskA[:], rb2, ALU.mult),
                         reads=[("mskA",), ("m8b",)], writes=[("c", mt2) for mt2 in range(NST)])

                for mt in range(NST):
                    sl = mt % 2
                    b = 8 * sl
                    c0, c1, c2 = (("l2", sl, j) for j in range(3))
                    xv = x_res[:, mt, :]
                    bb = 24 * sl
                    S.op("dve", lambda e, x=xv, bb=bb: e.bn_stats(bnr[:, bb:bb + 6], x[:, 0:512]),
                         reads=[("xr", mt)], writes=[("bnr2", sl, 0)])
                    S.op("dve", lambda e, x=xv, bb=bb: e.bn_stats(bnr[:, bb + 6:bb + 12], x[:, 512:1024]),
                         reads=[("xr", mt)], writes=[("bnr2", sl, 1)])
                    S.op("dve", lambda e, x=xv, bb=bb: e.bn_stats(bnr[:, bb + 12:bb + 18], x[:, 1024:1536]),
                         reads=[("xr", mt)], writes=[("bnr2", sl, 2)])
                    S.op("dve", lambda e, x=xv, bb=bb: e.bn_stats(bnr[:, bb + 18:bb + 24], x[:, 1536:2048]),
                         reads=[("xr", mt)], writes=[("bnr2", sl, 3)])
                    S.op("dve", lambda e, bb=bb, b=b: e.bn_aggr(st_ln[:, b:b + 2], bnr[:, bb:bb + 24]),
                         reads=[("bnr2", sl, 0), ("bnr2", sl, 1), ("bnr2", sl, 2), ("bnr2", sl, 3)],
                         writes=[c0])
                    S.op("act", lambda e, b=b: e.activation(st_ln[:, b + 2:b + 3], st_ln[:, b + 1:b + 2],
                                                            AF.Ln, bias=cst[:, 0:1], scale=1.0),
                         reads=[c0, ("cst",)], writes=[c1])
                    S.op("act", lambda e, b=b: e.activation(st_ln[:, b + 3:b + 4], st_ln[:, b + 2:b + 3],
                                                            AF.Exp, bias=cst[:, 1:2], scale=-0.5),
                         reads=[c1, ("cst",)], writes=[c1])
                    S.op("dve", lambda e, b=b: e.scalar_tensor_tensor(st_ln[:, b + 4:b + 5],
                                                                      st_ln[:, b:b + 1], -1.0,
                                                                      st_ln[:, b + 3:b + 4],
                                                                      ALU.mult, ALU.mult),
                         reads=[c0, c1], writes=[c2])
                    S.op("act", lambda e, x=xv, b=b:
                         e.activation(lnb2[:], x, AF.Identity, bias=st_ln[:, b + 4:b + 5],
                                      scale=st_ln[:, b + 3:b + 4]),
                         reads=[("xr", mt), c1, c2, ("lnb2",)], writes=[("lnb2",)])
                    for j in range(2):
                        fns = [lambda e, p=psL2[j], q=q, j=j:
                               e.transpose(p[:, q * 128:(q + 1) * 128],
                                           lnb2[:, (8 * j + q) * 128:(8 * j + q + 1) * 128], ident[:])
                               for q in range(8)]
                        S.group("pe", fns, reads=[("lnb2",), ("ident",)], writes=[("pl", j)])
                        S.op("dve", lambda e, p=psL2[j], mt=mt, j=j:
                             e.tensor_copy(X2T[:, 8 * j:8 * (j + 1), mt * 128:(mt + 1) * 128],
                                           p[:].rearrange("p (a b) -> p a b", b=128)),
                             reads=[("pl", j)], writes=[("x2t", mt, j)])
                    router(mt)
                topk_all()

            with ExitStack() as xp:
                psG = [xp.enter_context(nc.psum_tensor(f"psG{j}", [128, 512], F32)) for j in range(2)]
                psU = [xp.enter_context(nc.psum_tensor(f"psU{j}", [128, 512], F32)) for j in range(2)]
                psD = [xp.enter_context(nc.psum_tensor(f"psD{j}", [128, 512], F32)) for j in range(4)]

                first_grp = [True]
                for e_i in range(NE + 1):
                    shared = (e_i == NE)
                    for th in range(2):
                        for which, wbuf, wtok, ps in (("g", wg_b, ("wg",), psG), ("u", wu_b, ("wu",), psU)):
                            for mi in range(8):
                                p = ps[mi % 2]
                                ptok = (f"ps{which}", mi % 2)
                                fns = [lambda e, p=p, w=wbuf, ct=ct, mi=mi, th=th:
                                       e.matmul(p[:], w[:, ct, mi * 128:(mi + 1) * 128],
                                                X2T[:, ct, th * 512:(th + 1) * 512],
                                                start=(ct == 0), stop=(ct == NCT - 1))
                                       for ct in range(NCT)]
                                wr = [ptok]
                                if which == "u" and first_grp[0]:
                                    wr += [("psR",)]
                                    first_grp[0] = False
                                x2toks = [("x2t", mt2, j2) for mt2 in range(4 * th, 4 * th + 4) for j2 in range(2)]
                                S.group("pe", fns, reads=[wtok] + x2toks, writes=wr)
                                if which == "g":
                                    S.op("act", lambda e, p=p, mi=mi: e.activation(g_sb[:, mi, :], p[:], AF.Silu,
                                                                                   bias=cst[:, 1:2], scale=1.0),
                                         reads=[ptok, ("cst",)], writes=[("gsb", mi)])
                                else:
                                    S.op("dve", lambda e, p=p, mi=mi: e.tensor_tensor(g_sb[:, mi, :], p[:],
                                                                                      g_sb[:, mi, :], ALU.mult),
                                         reads=[ptok, ("gsb", mi)], writes=[("gsb", mi)])
                        if th == 1 and e_i <= NE - 1:
                            nsrc_g = d_swgt if e_i == NE - 1 else d_wgt[e_i + 1]
                            S.dma("pool", "wg", lambda e, s=nsrc_g: e.dma_start(out=wg_b[:], in_=s, max_dma_last_dim=MDL),
                                  writes=[("wg",)])
                        for mtl in range(4):
                            mt = th * 4 + mtl
                            for co in range(4):
                                bnk = (mtl * 4 + co) % 4
                                p = psD[bnk]
                                fns = [lambda e, p=p, ic=ic, mtl=mtl, co=co:
                                       e.matmul(p[:], g_sb[:, ic, mtl * 128:(mtl + 1) * 128],
                                                wd_b[:, ic, co * 512:(co + 1) * 512],
                                                start=(ic == 0), stop=(ic == 7))
                                       for ic in range(8)]
                                S.group("pe", fns,
                                        reads=[("gsb", ic) for ic in range(8)] + [("wd", 0), ("wd", 1)],
                                        writes=[("psD", bnk)])
                                if shared:
                                    S.op("dve", lambda e, p=p, mt=mt, co=co:
                                         e.tensor_tensor(x_res[:, mt, co * 512:(co + 1) * 512], p[:],
                                                         x_res[:, mt, co * 512:(co + 1) * 512], ALU.add),
                                         reads=[("psD", bnk), ("xr", mt)], writes=[("xr", mt)])
                                else:
                                    S.op("dve", lambda e, p=p, mt=mt, co=co, e_i=e_i:
                                         e.scalar_tensor_tensor(x_res[:, mt, co * 512:(co + 1) * 512], p[:],
                                                                c_all[:, mt, e_i:e_i + 1],
                                                                x_res[:, mt, co * 512:(co + 1) * 512],
                                                                ALU.mult, ALU.add),
                                         reads=[("psD", bnk), ("c", mt), ("xr", mt)], writes=[("xr", mt)])
                    if e_i <= NE - 1:
                        nsrc_u = d_swut if e_i == NE - 1 else d_wut[e_i + 1]
                        nsrc_d = d_swdt if e_i == NE - 1 else d_wdt[e_i + 1]
                        S.dma("pool", "wu", lambda e, s=nsrc_u: e.dma_start(out=wu_b[:], in_=s, max_dma_last_dim=MDL),
                              writes=[("wu",)])
                        for hh in range(2):
                            S.dma("pool", f"wd{hh}",
                                  lambda e, s=nsrc_d, hh=hh: e.dma_start(out=wd_b[:, 4 * hh:4 * (hh + 1), :],
                                                                         in_=s[:, 4 * hh:4 * (hh + 1), :],
                                                                         max_dma_last_dim=MDL),
                                  writes=[("wd", hh)])

                for mt in range(NST):
                    S.dma("sp", "out",
                          lambda e, mt=mt: e.dma_start(out=d_out[mt * 128:(mt + 1) * 128, :], in_=x_res[:, mt, :]),
                          reads=[("xr", mt)])
                S.tail_wait("sp", "out")
                S.flush(nc, semh)

    return nc



def core_rows(par):
    k = np.arange(NST)[:, None]
    i = np.arange(128)[None, :]
    return (256 * k + par + 2 * i).reshape(-1)


def build_mask(par):
    i = np.arange(128)[:, None]
    jj2 = np.arange(256)[None, :]
    valid = (jj2 ^ par) <= 2 * i + par
    return np.ascontiguousarray(np.where(valid, 0.0, -1.0e30).astype(np.float32))


def _pack(w):
    return np.ascontiguousarray(w)


_CACHE = {}


def prep_in_maps(L0_attn_x, L0_attn_Wq, L0_attn_bq, L0_attn_Wk, L0_attn_bk,
                 L0_attn_Wv, L0_attn_bv, L0_attn_Wo, L0_attn_bo,
                 L0_ffn_gate_w, L0_ffn_expert_bias, L0_ffn_w_gate, L0_ffn_w_up,
                 L0_ffn_w_down, L0_ffn_sw_gate, L0_ffn_sw_up, L0_ffn_sw_down):
    f = lambda a: np.ascontiguousarray(np.asarray(a, dtype=np.float32))
    x = f(L0_attn_x)
    wq, wk, wv = f(L0_attn_Wq), f(L0_attn_Wk), f(L0_attn_Wv)
    bq, bk, bv = f(L0_attn_bq), f(L0_attn_bk), f(L0_attn_bv)
    wo, bo = f(L0_attn_Wo), f(L0_attn_bo)
    gw, eb = f(L0_ffn_gate_w), f(L0_ffn_expert_bias)
    wg, wu, wd = f(L0_ffn_w_gate), f(L0_ffn_w_up), f(L0_ffn_w_down)
    swg, swu, swd = f(L0_ffn_sw_gate), f(L0_ffn_sw_up), f(L0_ffn_sw_down)

    def chunk2(a):
        return np.ascontiguousarray(a.reshape(a.shape[0] // 128, 128, -1).transpose(1, 0, 2))

    def chunk3(a):
        e, c, xx = a.shape
        return np.ascontiguousarray(a.reshape(e, c // 128, 128, xx).transpose(0, 2, 1, 3))

    common = {
        "wq": chunk3(wq), "wk": chunk3(wk), "wv": chunk3(wv),
        "bqt": np.ascontiguousarray(bq.T), "bkt": np.ascontiguousarray(bk.T),
        "bvt": np.ascontiguousarray(bv.T),
        "wo": chunk2(wo), "bo": np.ascontiguousarray(bo[None, :]),
        "gwt": chunk2(np.ascontiguousarray(gw.T)),
        "ebr": np.ascontiguousarray(eb[None, :]),
        "wgt": chunk3(np.ascontiguousarray(wg.transpose(0, 2, 1))),
        "wut": chunk3(np.ascontiguousarray(wu.transpose(0, 2, 1))),
        "wdt": chunk3(np.ascontiguousarray(wd.transpose(0, 2, 1))),
        "swgt": chunk2(np.ascontiguousarray(swg.T)),
        "swut": chunk2(np.ascontiguousarray(swu.T)),
        "swdt": chunk2(np.ascontiguousarray(swd.T)),
    }
    masks = {p: build_mask(p) for p in (0, 1)}
    perms = {p: (np.arange(2048) ^ p) for p in (0, 1)}

    in_maps = []
    for c in range(8):
        b, par = c // 2, c % 2
        m = dict(common)
        m["xkv"] = np.ascontiguousarray(x[b][perms[par]])
        m["mask"] = masks[par]
        in_maps.append(m)
    return in_maps


def kernel(**inputs):
    in_maps = prep_in_maps(**inputs)
    if "nc" not in _CACHE:
        _CACHE["nc"] = build_program()
    nc = _CACHE["nc"]
    res = run_bass_kernel_spmd(nc, in_maps, core_ids=list(range(8)))

    out = np.empty((4, 2048, C), np.float32)
    rows = {p: core_rows(p) for p in (0, 1)}
    for c in range(8):
        b, par = c // 2, c % 2
        out[b, rows[par]] = res.results[c]["out"]
    return out


# version 16
# speedup vs baseline: 1.1339x; 1.0376x over previous
import numpy as np
from contextlib import ExitStack

import concourse.bass as bass
import concourse.mybir as mybir
from concourse.bass_utils import run_bass_kernel_spmd

F32 = mybir.dt.float32
BF16 = mybir.dt.bfloat16
I32 = mybir.dt.int32
AF = mybir.ActivationFunctionType
ALU = mybir.AluOpType
AX = mybir.AxisListType

C = 2048
NCT = 16
NH = 16
HD = 128
NST = 8
TL = 1024
NE = 16
NI = 1024
TOPK = 4
EPS = 1e-5
SCALE = float(HD) ** -0.5
MDL = 8192

ENG = ("pe", "dve", "act", "pool", "sp")


class Sched:
    def __init__(self):
        self.items = {n: [] for n in ENG}
        self.waited = {n: {} for n in ENG}
        self.cnt = {}
        self.last = {}
        self.readers = {}
        self.dma_sems = set()

    def _need(self, reads, writes):
        need = {}

        def add(sem, val):
            if val > need.get(sem, 0):
                need[sem] = val

        for t in reads:
            lw = self.last.get(t)
            if lw:
                add(*lw)
        for t in writes:
            lw = self.last.get(t)
            if lw:
                add(*lw)
            for sem, val in self.readers.get(t, {}).items():
                add(sem, val)
        return need

    def _waits(self, st, need):
        for sem in sorted(need):
            val = need[sem]
            if self.waited[st].get(sem, 0) < val:
                self.items[st].append(("w", sem, val))
                self.waited[st][sem] = val

    def group(self, st, fns, reads=(), writes=()):
        assert st in ("pe", "dve", "act", "pool")
        self._waits(st, self._need(reads, writes))
        self.cnt[st] = self.cnt.get(st, 0) + 1
        v = self.cnt[st]
        for f in fns[:-1]:
            self.items[st].append(("op", f, None, 0))
        self.items[st].append(("op", fns[-1], st, 1))
        for t in writes:
            self.last[t] = (st, v)
            self.readers[t] = {}
        for t in reads:
            self.readers.setdefault(t, {})[st] = v

    def op(self, st, fn, reads=(), writes=()):
        self.group(st, [fn], reads, writes)

    def dma(self, q, sem, fn, reads=(), writes=()):
        assert q in ("sp", "pool")
        self.dma_sems.add(sem)
        self._waits(q, self._need(reads, writes))
        self.cnt[sem] = self.cnt.get(sem, 0) + 16
        v = self.cnt[sem]
        self.items[q].append(("op", fn, sem, 16))
        for t in writes:
            self.last[t] = (sem, v)
            self.readers[t] = {}
        for t in reads:
            self.readers.setdefault(t, {})[sem] = v

    def level(self, sem, tokens):
        for t in tokens:
            if t in self.last:
                self.last[t] = (sem, self.cnt.get(sem, 0))

    def snap(self):
        return {s: self.cnt.get(s, 0) for s in ("pe", "dve", "act", "pool")}

    def barrier_at(self, q, snap):
        self._waits(q, {s: v for s, v in snap.items() if v})

    def barrier_queue(self, q):
        need = {}
        for s in ("pe", "dve", "act", "pool"):
            if self.cnt.get(s, 0):
                need[s] = self.cnt[s]
        for s in self.dma_sems:
            if self.cnt.get(s, 0):
                need[s] = self.cnt[s]
        self._waits(q, need)

    def tail_wait(self, q, sem):
        self.waited[q].pop(sem, None)
        self._waits(q, {sem: self.cnt.get(sem, 0)})

    def flush(self, nc, semh):
        if not any(self.items[n] for n in ENG):
            return
        with nc.Block() as blk:
            for st, attr in (("pe", "tensor"), ("dve", "vector"),
                             ("act", "scalar"), ("pool", "gpsimd"),
                             ("sp", "sync")):
                items = self.items[st]
                self.items[st] = []
                if not items:
                    continue

                def mk(items=items):
                    def f(eng):
                        for it in items:
                            if it[0] == "w":
                                eng.wait_ge(semh[it[1]], it[2])
                            else:
                                inst = it[1](eng)
                                if it[2] is not None:
                                    inst.then_inc(semh[it[2]], it[3])
                    return f

                getattr(blk, attr)(mk())


ALL_SEMS = (["pe", "dve", "act", "pool", "sm", "smp", "xr", "gw", "bo",
             "xkv0", "xkv1", "xkv2", "wqb", "wkb", "wvb", "wo0", "wo1",
             "wg", "wu", "wd0", "wd1", "out"])


def build_program():
    nc = bass.Bass("TRN2", target_bir_lowering=False)

    d_xkv = nc.dram_tensor("xkv", [2048, C], F32, kind="ExternalInput").ap()
    d_mask = nc.dram_tensor("mask", [128, 256], F32, kind="ExternalInput").ap()
    d_wq = nc.dram_tensor("wq", [NH, 128, NCT, HD], F32, kind="ExternalInput").ap()
    d_wk = nc.dram_tensor("wk", [NH, 128, NCT, HD], F32, kind="ExternalInput").ap()
    d_wv = nc.dram_tensor("wv", [NH, 128, NCT, HD], F32, kind="ExternalInput").ap()
    d_bqt = nc.dram_tensor("bqt", [HD, NH], F32, kind="ExternalInput").ap()
    d_bkt = nc.dram_tensor("bkt", [HD, NH], F32, kind="ExternalInput").ap()
    d_bvt = nc.dram_tensor("bvt", [HD, NH], F32, kind="ExternalInput").ap()
    d_wo = nc.dram_tensor("wo", [128, NCT, C], F32, kind="ExternalInput").ap()
    d_bo = nc.dram_tensor("bo", [1, C], F32, kind="ExternalInput").ap()
    d_gwt = nc.dram_tensor("gwt", [128, NCT, NE], F32, kind="ExternalInput").ap()
    d_ebr = nc.dram_tensor("ebr", [1, NE], F32, kind="ExternalInput").ap()
    d_wgt = nc.dram_tensor("wgt", [NE, 128, NCT, NI], F32, kind="ExternalInput").ap()
    d_wut = nc.dram_tensor("wut", [NE, 128, NCT, NI], F32, kind="ExternalInput").ap()
    d_wdt = nc.dram_tensor("wdt", [NE, 128, 8, C], F32, kind="ExternalInput").ap()
    d_swgt = nc.dram_tensor("swgt", [128, NCT, NI], F32, kind="ExternalInput").ap()
    d_swut = nc.dram_tensor("swut", [128, NCT, NI], F32, kind="ExternalInput").ap()
    d_swdt = nc.dram_tensor("swdt", [128, 8, C], F32, kind="ExternalInput").ap()
    d_out = nc.dram_tensor("out", [TL, C], F32, kind="ExternalOutput").ap()

    S = Sched()

    with ExitStack() as top:
        semh = {n: top.enter_context(nc.semaphore(f"s_{n}")) for n in ALL_SEMS}
        E = top.enter_context

        x_res = E(nc.sbuf_tensor("x_res", [128, NST, C], F32))
        ident = E(nc.sbuf_tensor("ident", [128, 128], BF16))
        cst = E(nc.sbuf_tensor("cst", [128, 3], F32))
        onr = E(nc.sbuf_tensor("onr", [1, 128], BF16))
        st_ln = E(nc.sbuf_tensor("st_ln", [128, 16], F32))
        st_at = E(nc.sbuf_tensor("st_at", [128, 24], F32))
        bnr = E(nc.sbuf_tensor("bnr", [128, 48], F32))

        with ExitStack() as sAB:
            EB = sAB.enter_context
            YT = EB(nc.sbuf_tensor("YT", [128, NH, TL], BF16))

            with ExitStack() as s1:
                E1 = s1.enter_context
                ln1T = E1(nc.sbuf_tensor("ln1T", [128, NCT, 2048], BF16))
                wqb = E1(nc.sbuf_tensor("wqb", [128, NCT, HD], BF16))
                wkb = E1(nc.sbuf_tensor("wkb", [128, NCT, HD], BF16))
                wvb = E1(nc.sbuf_tensor("wvb", [128, NCT, HD], BF16))
                mask_sb = E1(nc.sbuf_tensor("mask_sb", [128, 256], BF16))
                bqt_s = E1(nc.sbuf_tensor("bqt_s", [HD, NH], F32))
                bkt_sb = E1(nc.sbuf_tensor("bkt_sb", [HD, NH], F32))
                bvt_sb = E1(nc.sbuf_tensor("bvt_sb", [HD, NH], F32))

                with ExitStack() as p1:
                    EP = p1.enter_context
                    io32 = EP(nc.sbuf_tensor("io32", [128, 128], I32))
                    bqt_r = EP(nc.sbuf_tensor("bqt_r", [HD, NH], F32))
                    xsl = [EP(nc.sbuf_tensor(f"xsl{s}", [128, 2048], F32)) for s in range(3)]
                    lnb = [EP(nc.sbuf_tensor(f"lnb{s}", [128, 2048], BF16)) for s in range(2)]
                    psLN = [EP(nc.psum_tensor(f"psLN{j}", [128, 1024], BF16)) for j in range(4)]

                    S.dma("pool", "smp", lambda e, o=mask_sb, i=d_mask: e.dma_start(out=o[:], in_=i[:]), writes=[("mask",)])
                    S.dma("sp", "sm", lambda e, o=bqt_r, i=d_bqt: e.dma_start(out=o[:], in_=i[:]), writes=[("bqtr",)])
                    S.dma("sp", "sm", lambda e, o=bkt_sb, i=d_bkt: e.dma_start(out=o[:], in_=i[:]), writes=[("bkt",)])
                    S.dma("sp", "sm", lambda e, o=bvt_sb, i=d_bvt: e.dma_start(out=o[:], in_=i[:]), writes=[("bvt",)])
                    S.level("sm", [("bqtr",), ("bkt",), ("bvt",)])
                    S.level("smp", [("mask",)])
                    S.op("pool", lambda e, a=io32: e.iota(a[:], [[1, 128]], base=0, channel_multiplier=-1),
                         writes=[("io32",)])
                    S.op("dve", lambda e, a=ident, b=io32: e.tensor_scalar(a[:], b[:], 0.0, None, ALU.is_equal),
                         reads=[("io32",)], writes=[("ident",)])
                    S.op("dve", lambda e, a=cst: e.memset(a[:, 0:1], EPS), writes=[("cst",)])
                    S.op("dve", lambda e, a=cst: e.memset(a[:, 1:2], 0.0), reads=[("cst",)], writes=[("cst",)])
                    S.op("dve", lambda e, a=cst: e.memset(a[:, 2:3], -20.0), reads=[("cst",)], writes=[("cst",)])
                    S.op("dve", lambda e, a=onr: e.memset(a[:], 1.0), writes=[("onr",)])
                    S.op("dve", lambda e, a=bqt_s, b=bqt_r: e.tensor_scalar(a[:], b[:], SCALE, None, ALU.mult),
                         reads=[("bqtr",)], writes=[("bqts",)])

                    for nm, dst, src in (("wqb", wqb, d_wq), ("wkb", wkb, d_wk), ("wvb", wvb, d_wv)):
                        S.dma("pool", nm,
                              lambda e, o=dst, i=src: e.dma_start(out=o[:], in_=i[0], max_dma_last_dim=MDL),
                              writes=[(nm,)])

                    def ln1_load(i):
                        s3 = i % 3
                        S.dma("sp", f"xkv{s3}",
                              lambda e, o=xsl[s3], i2=d_xkv, i=i: e.dma_start(out=o[:], in_=i2[i * 128:(i + 1) * 128, :]),
                              writes=[("xsl", s3)])

                    def ln1_stats(i):
                        sl = i % 2
                        s3 = i % 3
                        b = 8 * sl
                        bb = 24 * sl
                        for cc in range(4):
                            S.op("dve", lambda e, x=xsl[s3], bb=bb, cc=cc:
                                 e.bn_stats(bnr[:, bb + 6 * cc:bb + 6 * cc + 6], x[:, 512 * cc:512 * (cc + 1)]),
                                 reads=[("xsl", s3)], writes=[("bnr", sl, cc)])
                        S.op("dve", lambda e, bb=bb, b=b: e.bn_aggr(st_ln[:, b:b + 2], bnr[:, bb:bb + 24]),
                             reads=[("bnr", sl, 0), ("bnr", sl, 1), ("bnr", sl, 2), ("bnr", sl, 3)],
                             writes=[("lst", sl, 0)])
                        S.op("act", lambda e, b=b: e.activation(st_ln[:, b + 2:b + 3], st_ln[:, b + 1:b + 2],
                                                                AF.Ln, bias=cst[:, 0:1], scale=1.0),
                             reads=[("lst", sl, 0), ("cst",)], writes=[("lst", sl, 1)])
                        S.op("act", lambda e, b=b: e.activation(st_ln[:, b + 3:b + 4], st_ln[:, b + 2:b + 3],
                                                                AF.Exp, bias=cst[:, 1:2], scale=-0.5),
                             reads=[("lst", sl, 1), ("cst",)], writes=[("lst", sl, 1)])

                    def ln1_norm(i):
                        sl = i % 2
                        s3 = i % 3
                        b = 8 * sl
                        S.op("dve", lambda e, b=b: e.scalar_tensor_tensor(st_ln[:, b + 4:b + 5],
                                                                          st_ln[:, b:b + 1], -1.0,
                                                                          st_ln[:, b + 3:b + 4],
                                                                          ALU.mult, ALU.mult),
                             reads=[("lst", sl, 0), ("lst", sl, 1)], writes=[("lst", sl, 2)])
                        S.op("act", lambda e, o=lnb[sl], x=xsl[s3], b=b:
                             e.activation(o[:], x[:], AF.Identity, bias=st_ln[:, b + 4:b + 5],
                                          scale=st_ln[:, b + 3:b + 4]),
                             reads=[("xsl", s3), ("lst", sl, 1), ("lst", sl, 2)], writes=[("lnb", sl)])

                    def ln1_tp(i):
                        sl = i % 2
                        for j in range(2):
                            bj = 2 * (i % 2) + j
                            fns = []
                            for q in range(8):
                                fns.append(lambda e, p=psLN[bj], src=lnb[sl], q=q, j=j:
                                           e.transpose(p[:, q * 128:(q + 1) * 128],
                                                       src[:, (8 * j + q) * 128:(8 * j + q + 1) * 128], ident[:]))
                            S.group("pe", fns, reads=[("lnb", sl), ("ident",)], writes=[("psLN", bj)])

                    def ln1_ev(i):
                        for j in range(2):
                            bj = 2 * (i % 2) + j
                            S.op("dve", lambda e, p=psLN[bj], i=i, j=j:
                                 e.tensor_copy(ln1T[:, 8 * j:8 * (j + 1), i * 128:(i + 1) * 128],
                                               p[:].rearrange("p (a b) -> p a b", b=128)),
                                 reads=[("psLN", bj)], writes=[("lnT",)])

                    ln1_load(0)
                    ln1_load(1)
                    for i in range(16):
                        ln1_stats(i)
                        if i >= 1:
                            ln1_norm(i - 1)
                            ln1_tp(i - 1)
                            if i + 1 < 16:
                                ln1_load(i + 1)
                        if i >= 2:
                            ln1_ev(i - 2)
                    ln1_norm(15)
                    ln1_tp(15)
                    ln1_ev(14)
                    ln1_ev(15)

                    for k in range(NST):
                        S.dma("sp", "xr",
                              lambda e, o=x_res, i=d_xkv, k=k:
                              e.dma_start(out=o[:, k, :], in_=i[256 * k:256 * (k + 1):2, :]),
                              writes=[("xr", k)])
                    S.level("xr", [("xr", k) for k in range(NST)])



                    S.flush(nc, semh)

                with ExitStack() as at:
                    EA = at.enter_context
                    qt = EA(nc.sbuf_tensor("qt", [128, TL], BF16))
                    kt = EA(nc.sbuf_tensor("kt", [128, 2048], BF16))
                    vtt = EA(nc.sbuf_tensor("vtt", [128, 2048], BF16))
                    vsb = EA(nc.sbuf_tensor("vsb", [128, NCT, HD], BF16))
                    psb = [EA(nc.sbuf_tensor(f"psb{s}", [128, 2048], BF16)) for s in range(3)]
                    ptb = [EA(nc.sbuf_tensor(f"ptb{s}", [128, NCT, HD], BF16)) for s in range(2)]
                    psP = [EA(nc.psum_tensor(f"psP{j}", [128, 512], F32)) for j in range(2)]
                    psS = EA(nc.psum_tensor("psS", [128, 1536], F32))
                    psPT = [EA(nc.psum_tensor(f"psPT{j}", [128, 1024], BF16)) for j in range(2)]
                    psPV = EA(nc.psum_tensor("psPV", [128, 512], F32))

                    lnr = ln1T[:].rearrange("p c (k w) -> p c k w", w=256)
                    rb = [0]

                    def proj(h, wbuf, wtok, bias_ap, btok, scal, dst, dtok, nchunks, pp, stride2):
                        for ncx in range(nchunks):
                            p = psP[pp[0] % 2]
                            ptok = ("psP", pp[0] % 2)
                            pp[0] += 1
                            fns = []
                            for ct in range(NCT):
                                if stride2:
                                    rhs = lnr[:, ct, 4 * ncx:4 * ncx + 4, 0:256:2]
                                else:
                                    rhs = ln1T[:, ct, ncx * 512:(ncx + 1) * 512]
                                fns.append(lambda e, p=p, w=wbuf, ct=ct, rhs=rhs:
                                           e.matmul(p[:], w[:, ct, :], rhs,
                                                    start=(ct == 0), stop=(ct == NCT - 1)))
                            S.group("pe", fns, reads=[(wtok,), ("lnT",)], writes=[ptok])
                            S.op("act", lambda e, p=p, d=dst, ncx=ncx, b=bias_ap, sc=scal:
                                 e.activation(d[:, ncx * 512:(ncx + 1) * 512], p[:], AF.Identity,
                                              bias=b, scale=sc),
                                 reads=[ptok, btok], writes=[dtok + (ncx,)])

                    def s_front(k):
                        par = k % 2
                        pb = k % 3
                        base = 12 * par
                        L = 256 * (k + 1)
                        widths = [512] * (L // 512) + ([256] if L % 512 else [])
                        nchunks = len(widths)
                        off = 0
                        for ci, w in enumerate(widths):
                            bk = rb[0] % 3
                            rb[0] += 1
                            S.group("pe",
                                    [lambda e, bk=bk, w=w, off=off, k=k:
                                     e.matmul(psS[:, bk * 512:bk * 512 + w],
                                              qt[:, k * 128:(k + 1) * 128],
                                              kt[:, off:off + w], start=True, stop=True)],
                                    reads=[("qt", k // 4), ("kt", off // 512)], writes=[("psS", bk)])
                            if ci == nchunks - 1:
                                mo = bk * 512 + w - 256
                                if w == 512:
                                    S.op("dve", lambda e, bk=bk:
                                         e.tensor_scalar(psS[:, bk * 512:bk * 512 + 256],
                                                         psS[:, bk * 512:bk * 512 + 256],
                                                         90.0, None, ALU.min),
                                         reads=[("psS", bk)], writes=[("psS", bk)])
                                S.op("dve", lambda e, mo=mo:
                                     e.scalar_tensor_tensor(psS[:, mo:mo + 256], psS[:, mo:mo + 256],
                                                            90.0, mask_sb[:], ALU.min, ALU.add),
                                     reads=[("psS", bk), ("mask",)], writes=[("psS", bk)])
                            else:
                                S.op("dve", lambda e, bk=bk, w=w:
                                     e.tensor_scalar(psS[:, bk * 512:bk * 512 + w],
                                                     psS[:, bk * 512:bk * 512 + w],
                                                     90.0, None, ALU.min),
                                     reads=[("psS", bk)], writes=[("psS", bk)])
                            S.op("act", lambda e, bk=bk, w=w, off=off, base=base, ci=ci, pb=pb:
                                 e.activation(psb[pb][:, off:off + w], psS[:, bk * 512:bk * 512 + w], AF.Exp,
                                              bias=cst[:, 2:3], scale=1.0,
                                              accum_out=st_at[:, base + 5 + ci:base + 6 + ci]),
                                 reads=[("psS", bk), ("cst",)], writes=[("psb", pb), ("sum", par, ci)])
                            off += w

                    def s_back(k):
                        par = k % 2
                        pb = k % 3
                        base = 12 * par
                        L = 256 * (k + 1)
                        nchunks = len([512] * (L // 512) + ([256] if L % 512 else []))
                        S.op("dve", lambda e, base=base, nchunks=nchunks:
                             e.reduce_sum(st_at[:, base + 9:base + 10], st_at[:, base + 5:base + 5 + nchunks],
                                          axis=AX.X),
                             reads=[("sum", par, ci) for ci in range(nchunks)], writes=[("tsum", par)])
                        S.op("dve", lambda e, base=base: e.reciprocal(st_at[:, base + 10:base + 11],
                                                                     st_at[:, base + 9:base + 10]),
                             reads=[("tsum", par)], writes=[("at2", par)])
                        S.op("dve", lambda e, L=L, base=base, pb=pb:
                             e.tensor_scalar(psb[pb][:, 0:L], psb[pb][:, 0:L],
                                             st_at[:, base + 10:base + 11], None, ALU.mult),
                             reads=[("at2", par), ("psb", pb)], writes=[("psb", pb)])

                    def pt_part(k):
                        pb = k % 3
                        tb = k % 2
                        nkc = 2 * (k + 1)
                        for ri, r in enumerate(range(0, nkc, 8)):
                            pt = psPT[ri % 2]
                            pttok = ("psPT", ri % 2)
                            n = min(8, nkc - r)
                            fns = [lambda e, j=j, r=r, pb=pb, pt=pt:
                                   e.transpose(pt[:, (j - r) * 128:(j - r + 1) * 128],
                                               psb[pb][:, j * 128:(j + 1) * 128], ident[:])
                                   for j in range(r, r + n)]
                            S.group("pe", fns, reads=[("psb", pb), ("ident",)], writes=[pttok])
                            S.op("act", lambda e, r=r, n=n, tb=tb, pt=pt:
                                 e.copy(ptb[tb][:, r:r + n, :],
                                        pt[:, 0:n * 128].rearrange("p (a b) -> p a b", b=128)),
                                 reads=[pttok], writes=[("ptb", tb)])

                    def pv_part(k, h):
                        tb = k % 2
                        nkc = 2 * (k + 1)
                        fns = [lambda e, j=j, k=k, nkc=nkc, tb=tb:
                               e.matmul(psPV[:, (k % 4) * 128:(k % 4 + 1) * 128],
                                        vsb[:, j, :], ptb[tb][:, j, :],
                                        start=(j == 0), stop=(j == nkc - 1))
                               for j in range(nkc)]
                        S.group("pe", fns, reads=[("vsb",), ("ptb", tb)], writes=[("psPV",)])
                        if k % 4 == 3:
                            S.op("act", lambda e, h=h, k=k:
                                 e.copy(YT[:, h, (k - 3) * 128:(k + 1) * 128], psPV[:]),
                                 reads=[("psPV",)], writes=[("yt",)])

                    vtt_toks = [("vtt", j) for j in range(4)]

                    def projall(h):
                        pp = [0]
                        proj(h, wqb, "wqb", bqt_s[:, h:h + 1], ("bqts",), SCALE, qt, ("qt",), 2, pp, True)
                        proj(h, wkb, "wkb", bkt_sb[:, h:h + 1], ("bkt",), 1.0, kt, ("kt",), 4, pp, False)
                        proj(h, wvb, "wvb", bvt_sb[:, h:h + 1], ("bvt",), 1.0, vtt, ("vtt",), 4, pp, False)

                    def vtrans(h):
                        for ri, r in enumerate(range(0, 16, 8)):
                            pt = psPT[ri % 2]
                            pttok = ("psPT", ri % 2)
                            fns = [lambda e, j=j, r=r, pt=pt: e.transpose(pt[:, (j - r) * 128:(j - r + 1) * 128],
                                                                          vtt[:, j * 128:(j + 1) * 128], ident[:])
                                   for j in range(r, r + 8)]
                            S.group("pe", fns, reads=vtt_toks + [("ident",)], writes=[pttok])
                            S.op("act", lambda e, r=r, pt=pt: e.copy(vsb[:, r:r + 8, :],
                                                                     pt[:].rearrange("p (a b) -> p a b", b=128)),
                                 reads=[pttok], writes=[("vsb",)])

                    def prefetch(h):
                        for nm, dst, src_ in (("wqb", wqb, d_wq), ("wkb", wkb, d_wk), ("wvb", wvb, d_wv)):
                            S.dma("pool", nm,
                                  lambda e, o=dst, i=src_, h=h: e.dma_start(out=o[:], in_=i[h],
                                                                            max_dma_last_dim=MDL),
                                  writes=[(nm,)])

                    projall(0)
                    vtrans(0)
                    prefetch(1)
                    for h in range(NH):
                        for k in range(NST):
                            s_front(k)
                            if k >= 1:
                                s_back(k - 1)
                            if k >= 2:
                                pt_part(k - 2)
                            if k >= 3:
                                pv_part(k - 3, h)
                        s_back(7)
                        if h + 1 < NH:
                            projall(h + 1)
                            if h + 2 < NH:
                                prefetch(h + 2)
                        pt_part(6)
                        pv_part(5, h)
                        pt_part(7)
                        pv_part(6, h)
                        pv_part(7, h)
                        if h + 1 < NH:
                            vtrans(h + 1)
                    S.flush(nc, semh)

            attn_snap = S.snap()
            wg_b = E(nc.sbuf_tensor("wg_b", [128, NCT, NI], BF16, side="right"))
            wu_b = E(nc.sbuf_tensor("wu_b", [128, NCT, NI], BF16, side="right"))
            with ExitStack() as wp:
                EW = wp.enter_context
                wo_b = [EW(nc.sbuf_tensor(f"wo_b{s}", [128, NCT, 512], BF16)) for s in range(2)]
                bo_bf = EW(nc.sbuf_tensor("bo_bf", [1, C], BF16))
                psW = [EW(nc.psum_tensor(f"psW{m}", [128, 512], F32)) for m in range(NST)]

                S.barrier_at("pool", attn_snap)
                S.dma("pool", "bo", lambda e: e.dma_start(out=bo_bf[:], in_=d_bo[:], max_dma_last_dim=MDL),
                      writes=[("bo",)])
                for co in range(2):
                    S.dma("pool", f"wo{co}",
                          lambda e, o=wo_b[co], co=co: e.dma_start(out=o[:], in_=d_wo[:, :, co * 512:(co + 1) * 512],
                                                                   max_dma_last_dim=MDL),
                          writes=[("wob", co)])
                S.dma("pool", "wg", lambda e: e.dma_start(out=wg_b[:], in_=d_wgt[0], max_dma_last_dim=MDL),
                      writes=[("wg",)])
                S.dma("pool", "wu", lambda e: e.dma_start(out=wu_b[:], in_=d_wut[0], max_dma_last_dim=MDL),
                      writes=[("wu",)])
                for co in range(4):
                    sl = co % 2
                    if co >= 2:
                        S.dma("pool", f"wo{sl}",
                              lambda e, o=wo_b[sl], co=co: e.dma_start(out=o[:], in_=d_wo[:, :, co * 512:(co + 1) * 512],
                                                                       max_dma_last_dim=MDL),
                              writes=[("wob", sl)])
                    for m in range(NST):
                        fns = [lambda e, m=m, ct=ct, sl=sl: e.matmul(psW[m][:], YT[:, ct, m * 128:(m + 1) * 128],
                                                                     wo_b[sl][:, ct, :],
                                                                     start=(ct == 0), stop=False)
                               for ct in range(NCT)]
                        fns.append(lambda e, m=m, co=co: e.matmul(psW[m][:], onr[0:1, :],
                                                                  bo_bf[0:1, co * 512:(co + 1) * 512],
                                                                  start=False, stop=True))
                        S.group("pe", fns, reads=[("yt",), ("wob", sl), ("onr",), ("bo",)],
                                writes=[("psW", m)])
                        S.op("dve", lambda e, m=m, co=co:
                             e.tensor_tensor(x_res[:, m, co * 512:(co + 1) * 512], psW[m][:],
                                             x_res[:, m, co * 512:(co + 1) * 512], ALU.add),
                             reads=[("psW", m), ("xr", m)], writes=[("xr", m)])
                S.flush(nc, semh)

        with ExitStack() as fp:
            EF = fp.enter_context
            X2T = EF(nc.sbuf_tensor("X2T", [128, NCT, TL], BF16))
            wd_b = EF(nc.sbuf_tensor("wd_b", [128, 8, C], BF16))
            g_sb = EF(nc.sbuf_tensor("g_sb", [128, 8, 512], BF16))
            lnb2 = EF(nc.sbuf_tensor("lnb2", [128, 2048], BF16))
            gw_b = EF(nc.sbuf_tensor("gw_b", [128, NCT, NE], BF16))
            ebr_sb = EF(nc.sbuf_tensor("ebr_sb", [1, NE], BF16))
            ebbc = EF(nc.sbuf_tensor("ebbc", [128, NE], F32))
            sgb = EF(nc.sbuf_tensor("sgb", [128, NST, NE], F32))
            c_all = EF(nc.sbuf_tensor("c_all", [128, NST, NE], F32))
            mskA = EF(nc.sbuf_tensor("mskA", [128, NST, NE], F32))
            maccA = EF(nc.sbuf_tensor("maccA", [128, NST, NE], F32))
            wrkA = c_all
            m8a = EF(nc.sbuf_tensor("m8a", [128, NST], F32))
            m8b = EF(nc.sbuf_tensor("m8b", [128, NST], F32))

            S.barrier_queue("pool")
            S.barrier_queue("sp")
            S.dma("pool", "gw", lambda e: e.dma_start(out=gw_b[:], in_=d_gwt[:], max_dma_last_dim=MDL),
                  writes=[("gw",)])
            for hh in range(2):
                S.dma("pool", f"wd{hh}",
                      lambda e, hh=hh: e.dma_start(out=wd_b[:, 4 * hh:4 * (hh + 1), :],
                                                   in_=d_wdt[0][:, 4 * hh:4 * (hh + 1), :],
                                                   max_dma_last_dim=MDL),
                      writes=[("wd", hh)])
            S.dma("pool", "smp", lambda e: e.dma_start(out=ebr_sb[:], in_=d_ebr[:], max_dma_last_dim=MDL),
                  writes=[("ebr",)])
            S.level("smp", [("ebr",)])

            with ExitStack() as rp:
                psL2 = [rp.enter_context(nc.psum_tensor(f"psL2{j}", [128, 1024], BF16)) for j in range(2)]
                psR = rp.enter_context(nc.psum_tensor("psR", [128, NE], F32))

                S.op("pe", lambda e: e.matmul(psR[:], onr[0:1, :], ebr_sb[0:1, :], start=True, stop=True),
                     reads=[("onr",), ("ebr",)], writes=[("psR",)])
                S.op("dve", lambda e: e.tensor_copy(ebbc[:], psR[:]), reads=[("psR",)], writes=[("ebbc",)])

                def router(mt):
                    fns = [lambda e, ct=ct, mt=mt: e.matmul(psR[:], X2T[:, ct, mt * 128:(mt + 1) * 128],
                                                            gw_b[:, ct, :], start=(ct == 0), stop=(ct == NCT - 1))
                           for ct in range(NCT)]
                    S.group("pe", fns, reads=[("x2t", mt, 0), ("x2t", mt, 1), ("gw",)], writes=[("psR",)])
                    S.op("act", lambda e, mt=mt: e.activation(sgb[:, mt, :], psR[:], AF.Sigmoid,
                                                              bias=cst[:, 1:2], scale=1.0),
                         reads=[("psR",), ("cst",)], writes=[("sg", mt)])

                def topk_all():
                    sg_toks = [("sg", mt) for mt in range(NST)]
                    ebb = ebbc[:].rearrange("p b -> p () b").broadcast_to([128, NST, NE])
                    S.op("dve", lambda e, ebb=ebb: e.tensor_tensor(wrkA[:], sgb[:], ebb, ALU.add),
                         reads=sg_toks + [("ebbc",)], writes=[("wrkA",)])
                    S.op("dve", lambda e: e.memset(maccA[:], 0.0), writes=[("maccA",)])
                    for it in range(TOPK):
                        S.op("dve", lambda e: e.reduce_max(m8a[:], wrkA[:], axis=AX.X),
                             reads=[("wrkA",)], writes=[("m8a",)])
                        mb = m8a[:].rearrange("p a -> p a ()").broadcast_to([128, NST, NE])
                        S.op("dve", lambda e, mb=mb: e.tensor_tensor(mskA[:], wrkA[:], mb, ALU.is_ge),
                             reads=[("wrkA",), ("m8a",), ("mskA",)], writes=[("mskA",)])
                        S.op("dve", lambda e: e.tensor_tensor(maccA[:], maccA[:], mskA[:], ALU.add),
                             reads=[("maccA",), ("mskA",)], writes=[("maccA",)])
                        S.op("dve", lambda e: e.scalar_tensor_tensor(wrkA[:], mskA[:], -1.0e30, wrkA[:],
                                                                     ALU.mult, ALU.add),
                             reads=[("mskA",), ("wrkA",)], writes=[("wrkA",)])
                    S.op("dve", lambda e: e.tensor_tensor(mskA[:], sgb[:], maccA[:], ALU.mult),
                         reads=sg_toks + [("maccA",), ("mskA",)], writes=[("mskA",)])
                    S.op("dve", lambda e: e.reduce_sum(m8a[:], mskA[:], axis=AX.X),
                         reads=[("mskA",)], writes=[("m8a",)])
                    S.op("dve", lambda e: e.reciprocal(m8b[:], m8a[:]),
                         reads=[("m8a",)], writes=[("m8b",)])
                    rb2 = m8b[:].rearrange("p a -> p a ()").broadcast_to([128, NST, NE])
                    S.op("dve", lambda e, rb2=rb2: e.tensor_scalar(c_all[:], mskA[:], None, None, ALU.bypass)
                         if False else e.tensor_tensor(c_all[:], mskA[:], rb2, ALU.mult),
                         reads=[("mskA",), ("m8b",)], writes=[("c", mt2) for mt2 in range(NST)])

                for mt in range(NST):
                    sl = mt % 2
                    b = 8 * sl
                    c0, c1, c2 = (("l2", sl, j) for j in range(3))
                    xv = x_res[:, mt, :]
                    bb = 24 * sl
                    S.op("dve", lambda e, x=xv, bb=bb: e.bn_stats(bnr[:, bb:bb + 6], x[:, 0:512]),
                         reads=[("xr", mt)], writes=[("bnr2", sl, 0)])
                    S.op("dve", lambda e, x=xv, bb=bb: e.bn_stats(bnr[:, bb + 6:bb + 12], x[:, 512:1024]),
                         reads=[("xr", mt)], writes=[("bnr2", sl, 1)])
                    S.op("dve", lambda e, x=xv, bb=bb: e.bn_stats(bnr[:, bb + 12:bb + 18], x[:, 1024:1536]),
                         reads=[("xr", mt)], writes=[("bnr2", sl, 2)])
                    S.op("dve", lambda e, x=xv, bb=bb: e.bn_stats(bnr[:, bb + 18:bb + 24], x[:, 1536:2048]),
                         reads=[("xr", mt)], writes=[("bnr2", sl, 3)])
                    S.op("dve", lambda e, bb=bb, b=b: e.bn_aggr(st_ln[:, b:b + 2], bnr[:, bb:bb + 24]),
                         reads=[("bnr2", sl, 0), ("bnr2", sl, 1), ("bnr2", sl, 2), ("bnr2", sl, 3)],
                         writes=[c0])
                    S.op("act", lambda e, b=b: e.activation(st_ln[:, b + 2:b + 3], st_ln[:, b + 1:b + 2],
                                                            AF.Ln, bias=cst[:, 0:1], scale=1.0),
                         reads=[c0, ("cst",)], writes=[c1])
                    S.op("act", lambda e, b=b: e.activation(st_ln[:, b + 3:b + 4], st_ln[:, b + 2:b + 3],
                                                            AF.Exp, bias=cst[:, 1:2], scale=-0.5),
                         reads=[c1, ("cst",)], writes=[c1])
                    S.op("dve", lambda e, b=b: e.scalar_tensor_tensor(st_ln[:, b + 4:b + 5],
                                                                      st_ln[:, b:b + 1], -1.0,
                                                                      st_ln[:, b + 3:b + 4],
                                                                      ALU.mult, ALU.mult),
                         reads=[c0, c1], writes=[c2])
                    S.op("act", lambda e, x=xv, b=b:
                         e.activation(lnb2[:], x, AF.Identity, bias=st_ln[:, b + 4:b + 5],
                                      scale=st_ln[:, b + 3:b + 4]),
                         reads=[("xr", mt), c1, c2, ("lnb2",)], writes=[("lnb2",)])
                    for j in range(2):
                        fns = [lambda e, p=psL2[j], q=q, j=j:
                               e.transpose(p[:, q * 128:(q + 1) * 128],
                                           lnb2[:, (8 * j + q) * 128:(8 * j + q + 1) * 128], ident[:])
                               for q in range(8)]
                        S.group("pe", fns, reads=[("lnb2",), ("ident",)], writes=[("pl", j)])
                        S.op("dve", lambda e, p=psL2[j], mt=mt, j=j:
                             e.tensor_copy(X2T[:, 8 * j:8 * (j + 1), mt * 128:(mt + 1) * 128],
                                           p[:].rearrange("p (a b) -> p a b", b=128)),
                             reads=[("pl", j)], writes=[("x2t", mt, j)])
                    router(mt)
                topk_all()

            with ExitStack() as xp:
                psG = [xp.enter_context(nc.psum_tensor(f"psG{j}", [128, 512], F32)) for j in range(2)]
                psU = [xp.enter_context(nc.psum_tensor(f"psU{j}", [128, 512], F32)) for j in range(2)]
                psD = [xp.enter_context(nc.psum_tensor(f"psD{j}", [128, 512], F32)) for j in range(4)]

                first_grp = [True]
                for e_i in range(NE + 1):
                    shared = (e_i == NE)
                    for th in range(2):
                        for which, wbuf, wtok, ps in (("g", wg_b, ("wg",), psG), ("u", wu_b, ("wu",), psU)):
                            for mi in range(8):
                                p = ps[mi % 2]
                                ptok = (f"ps{which}", mi % 2)
                                fns = [lambda e, p=p, w=wbuf, ct=ct, mi=mi, th=th:
                                       e.matmul(p[:], w[:, ct, mi * 128:(mi + 1) * 128],
                                                X2T[:, ct, th * 512:(th + 1) * 512],
                                                start=(ct == 0), stop=(ct == NCT - 1))
                                       for ct in range(NCT)]
                                wr = [ptok]
                                if which == "u" and first_grp[0]:
                                    wr += [("psR",)]
                                    first_grp[0] = False
                                x2toks = [("x2t", mt2, j2) for mt2 in range(4 * th, 4 * th + 4) for j2 in range(2)]
                                S.group("pe", fns, reads=[wtok] + x2toks, writes=wr)
                                if which == "g":
                                    S.op("act", lambda e, p=p, mi=mi: e.activation(g_sb[:, mi, :], p[:], AF.Silu,
                                                                                   bias=cst[:, 1:2], scale=1.0),
                                         reads=[ptok, ("cst",)], writes=[("gsb", mi)])
                                else:
                                    S.op("dve", lambda e, p=p, mi=mi: e.tensor_tensor(g_sb[:, mi, :], p[:],
                                                                                      g_sb[:, mi, :], ALU.mult),
                                         reads=[ptok, ("gsb", mi)], writes=[("gsb", mi)])
                        if th == 1 and e_i <= NE - 1:
                            nsrc_g = d_swgt if e_i == NE - 1 else d_wgt[e_i + 1]
                            S.dma("pool", "wg", lambda e, s=nsrc_g: e.dma_start(out=wg_b[:], in_=s, max_dma_last_dim=MDL),
                                  writes=[("wg",)])
                        for mtl in range(4):
                            mt = th * 4 + mtl
                            for co in range(4):
                                bnk = (mtl * 4 + co) % 4
                                p = psD[bnk]
                                fns = [lambda e, p=p, ic=ic, mtl=mtl, co=co:
                                       e.matmul(p[:], g_sb[:, ic, mtl * 128:(mtl + 1) * 128],
                                                wd_b[:, ic, co * 512:(co + 1) * 512],
                                                start=(ic == 0), stop=(ic == 7))
                                       for ic in range(8)]
                                S.group("pe", fns,
                                        reads=[("gsb", ic) for ic in range(8)] + [("wd", 0), ("wd", 1)],
                                        writes=[("psD", bnk)])
                                if shared:
                                    S.op("dve", lambda e, p=p, mt=mt, co=co:
                                         e.tensor_tensor(x_res[:, mt, co * 512:(co + 1) * 512], p[:],
                                                         x_res[:, mt, co * 512:(co + 1) * 512], ALU.add),
                                         reads=[("psD", bnk), ("xr", mt)], writes=[("xr", mt)])
                                else:
                                    S.op("dve", lambda e, p=p, mt=mt, co=co, e_i=e_i:
                                         e.scalar_tensor_tensor(x_res[:, mt, co * 512:(co + 1) * 512], p[:],
                                                                c_all[:, mt, e_i:e_i + 1],
                                                                x_res[:, mt, co * 512:(co + 1) * 512],
                                                                ALU.mult, ALU.add),
                                         reads=[("psD", bnk), ("c", mt), ("xr", mt)], writes=[("xr", mt)])
                    if e_i <= NE - 1:
                        nsrc_u = d_swut if e_i == NE - 1 else d_wut[e_i + 1]
                        nsrc_d = d_swdt if e_i == NE - 1 else d_wdt[e_i + 1]
                        S.dma("pool", "wu", lambda e, s=nsrc_u: e.dma_start(out=wu_b[:], in_=s, max_dma_last_dim=MDL),
                              writes=[("wu",)])
                        for hh in range(2):
                            S.dma("pool", f"wd{hh}",
                                  lambda e, s=nsrc_d, hh=hh: e.dma_start(out=wd_b[:, 4 * hh:4 * (hh + 1), :],
                                                                         in_=s[:, 4 * hh:4 * (hh + 1), :],
                                                                         max_dma_last_dim=MDL),
                                  writes=[("wd", hh)])

                for mt in range(NST):
                    S.dma("sp", "out",
                          lambda e, mt=mt: e.dma_start(out=d_out[mt * 128:(mt + 1) * 128, :], in_=x_res[:, mt, :]),
                          reads=[("xr", mt)])
                S.tail_wait("sp", "out")
                S.flush(nc, semh)

    return nc



def core_rows(par):
    k = np.arange(NST)[:, None]
    i = np.arange(128)[None, :]
    return (256 * k + par + 2 * i).reshape(-1)


def build_mask(par):
    i = np.arange(128)[:, None]
    jj2 = np.arange(256)[None, :]
    valid = (jj2 ^ par) <= 2 * i + par
    return np.ascontiguousarray(np.where(valid, 0.0, -1.0e30).astype(np.float32))


def _pack(w):
    return np.ascontiguousarray(w)


_CACHE = {}


def prep_in_maps(L0_attn_x, L0_attn_Wq, L0_attn_bq, L0_attn_Wk, L0_attn_bk,
                 L0_attn_Wv, L0_attn_bv, L0_attn_Wo, L0_attn_bo,
                 L0_ffn_gate_w, L0_ffn_expert_bias, L0_ffn_w_gate, L0_ffn_w_up,
                 L0_ffn_w_down, L0_ffn_sw_gate, L0_ffn_sw_up, L0_ffn_sw_down):
    f = lambda a: np.ascontiguousarray(np.asarray(a, dtype=np.float32))
    x = f(L0_attn_x)
    wq, wk, wv = f(L0_attn_Wq), f(L0_attn_Wk), f(L0_attn_Wv)
    bq, bk, bv = f(L0_attn_bq), f(L0_attn_bk), f(L0_attn_bv)
    wo, bo = f(L0_attn_Wo), f(L0_attn_bo)
    gw, eb = f(L0_ffn_gate_w), f(L0_ffn_expert_bias)
    wg, wu, wd = f(L0_ffn_w_gate), f(L0_ffn_w_up), f(L0_ffn_w_down)
    swg, swu, swd = f(L0_ffn_sw_gate), f(L0_ffn_sw_up), f(L0_ffn_sw_down)

    def chunk2(a):
        return np.ascontiguousarray(a.reshape(a.shape[0] // 128, 128, -1).transpose(1, 0, 2))

    def chunk3(a):
        e, c, xx = a.shape
        return np.ascontiguousarray(a.reshape(e, c // 128, 128, xx).transpose(0, 2, 1, 3))

    common = {
        "wq": chunk3(wq), "wk": chunk3(wk), "wv": chunk3(wv),
        "bqt": np.ascontiguousarray(bq.T), "bkt": np.ascontiguousarray(bk.T),
        "bvt": np.ascontiguousarray(bv.T),
        "wo": chunk2(wo), "bo": np.ascontiguousarray(bo[None, :]),
        "gwt": chunk2(np.ascontiguousarray(gw.T)),
        "ebr": np.ascontiguousarray(eb[None, :]),
        "wgt": chunk3(np.ascontiguousarray(wg.transpose(0, 2, 1))),
        "wut": chunk3(np.ascontiguousarray(wu.transpose(0, 2, 1))),
        "wdt": chunk3(np.ascontiguousarray(wd.transpose(0, 2, 1))),
        "swgt": chunk2(np.ascontiguousarray(swg.T)),
        "swut": chunk2(np.ascontiguousarray(swu.T)),
        "swdt": chunk2(np.ascontiguousarray(swd.T)),
    }
    masks = {p: build_mask(p) for p in (0, 1)}
    perms = {p: (np.arange(2048) ^ p) for p in (0, 1)}

    in_maps = []
    for c in range(8):
        b, par = c // 2, c % 2
        m = dict(common)
        m["xkv"] = np.ascontiguousarray(x[b][perms[par]])
        m["mask"] = masks[par]
        in_maps.append(m)
    return in_maps


def kernel(**inputs):
    in_maps = prep_in_maps(**inputs)
    if "nc" not in _CACHE:
        _CACHE["nc"] = build_program()
    nc = _CACHE["nc"]
    res = run_bass_kernel_spmd(nc, in_maps, core_ids=list(range(8)))

    out = np.empty((4, 2048, C), np.float32)
    rows = {p: core_rows(p) for p in (0, 1)}
    for c in range(8):
        b, par = c // 2, c % 2
        out[b, rows[par]] = res.results[c]["out"]
    return out
